# version 05
# baseline (speedup 1.0000x reference)
import numpy as np
import ml_dtypes
import concourse.bass as bass
import concourse.mybir as mybir
from concourse.bass_utils import run_bass_kernel_spmd

F32 = mybir.dt.float32
BF16 = mybir.dt.bfloat16
I32 = mybir.dt.int32
I8 = mybir.dt.int8
AF = mybir.ActivationFunctionType
ALU = mybir.AluOpType
AX = mybir.AxisListType
BF16NP = ml_dtypes.bfloat16

B, S, D = 4, 4096, 4096
NH, NKV, HD = 32, 8, 128
GS = 128
NSLOT, QR = 16, 128
THETA = 10000.0
EPS = 1e-5
ISQ = 1.0 / float(np.sqrt(HD))
NEG = -30000.0

KVT = S // 128
QT_ = 2048 // 128


def _ext(j):
    return 256 * (j + 1)



def _rope_tables():
    inv_freq = 1.0 / THETA ** (np.arange(0, HD, 2, dtype=np.float32) / HD)
    pos = np.arange(S, dtype=np.float32)[:, None]
    freqs = pos * inv_freq[None, :]
    emb = np.concatenate([freqs, freqs], axis=-1)
    return np.cos(emb).astype(np.float32), np.sin(emb).astype(np.float32)


def _host_consts(p):
    cos, sin = _rope_tables()
    sgn = np.where(np.arange(HD) < HD // 2, -1.0, 1.0).astype(np.float32)
    sin_adj = sin * sgn[None, :]
    cosk = np.ascontiguousarray(cos.T)
    sink = np.ascontiguousarray(sin_adj.T)
    qpos = np.concatenate([np.arange((2 * j + p) * 128, (2 * j + p) * 128 + 128)
                           for j in range(NSLOT)])
    cosq = np.ascontiguousarray(cos[qpos].T)
    sinq = np.ascontiguousarray(sin_adj[qpos].T)
    dmask = np.zeros((NSLOT, 128, 512), dtype=np.float32)
    for j in range(NSLOT):
        kcols = np.arange(_ext(j) - 256, _ext(j))
        qrows = (2 * j + p) * 128 + np.arange(128)
        dmask[j, :, 256:] = np.where(kcols[None, :] <= qrows[:, None], 0.0, NEG * ISQ)
    ident = np.eye(128, dtype=np.float32).astype(BF16NP)
    return cosk, sink, cosq, sinq, dmask.astype(BF16NP), ident



class Sem:

    def __init__(self, stack, nc, name):
        self.h = stack.enter_context(nc.semaphore(name))
        self.n = 0

    def bump(self, k=1):
        self.n += k
        return self.n


class Arena:

    def __init__(self, t, nbytes):
        self.t = t
        self.nbytes = nbytes
        self.off = 0
        self.marks = []

    def push(self):
        self.marks.append(self.off)

    def pop(self):
        self.off = self.marks.pop()

    def take(self, shape, dt):
        esz = {F32: 4, BF16: 2, I32: 4, I8: 1}[dt]
        n = int(np.prod(shape[1:])) * esz
        off = (self.off + 511) & ~511
        assert off + n <= self.nbytes, f"SBUF arena overflow: {off + n} > {self.nbytes}"
        self.off = off + n
        v = self.t[:, off:off + n].bitcast(dt)
        if len(shape) > 2:
            pat = " ".join(f"d{i}" for i in range(1, len(shape)))
            kw = {f"d{i}": shape[i] for i in range(1, len(shape))}
            v = v.rearrange(f"p ({pat}) -> p {pat}", **kw)
        if shape[0] != 128:
            v = v[0:shape[0]]
        return v


class Waits:

    def __init__(self):
        self.seen = {}

    def wait(self, eng, key, sem, val):
        if val <= 0:
            return
        if self.seen.get((key, id(sem)), -1) >= val:
            return
        self.seen[(key, id(sem))] = val
        eng.wait_ge(sem.h, val)



def build_nc(debug=False, serialize=False, ser_phases=()):
    import contextlib
    nc = bass.Bass("TRN2", target_bir_lowering=False)

    def dram(name, shape, dt, kind):
        return nc.dram_tensor(name, list(shape), dt, kind=kind).ap()

    scratch_kind = "ExternalOutput" if debug else "Internal"

    xq = dram("xq", (2048, D), F32, "ExternalInput")
    xkv = dram("xkv", (S, D), F32, "ExternalInput")
    qwq = dram("qwq", (D, D), I32, "ExternalInput")
    qwk = dram("qwk", (D, 1024), I32, "ExternalInput")
    qwv = dram("qwv", (D, 1024), I32, "ExternalInput")
    qwo = dram("qwo", (D, D), I32, "ExternalInput")
    sq = dram("sq", (D, D), F32, "ExternalInput")
    sk = dram("sk", (D, 1024), F32, "ExternalInput")
    sv = dram("sv", (D, 1024), F32, "ExternalInput")
    so = dram("so", (D, D), F32, "ExternalInput")
    fq3 = dram("fq3", (524288, 12), I8, "ExternalInput")
    fnorm = dram("fnorm", (D, 128), F32, "ExternalInput")
    fbias = dram("fbias", (128, D), F32, "ExternalInput")
    cosk = dram("cosk", (HD, S), F32, "ExternalInput")
    sink = dram("sink", (HD, S), F32, "ExternalInput")
    cosq = dram("cosq", (HD, 2048), F32, "ExternalInput")
    sinq = dram("sinq", (HD, 2048), F32, "ExternalInput")
    dmask = dram("dmask", (NSLOT, 128, 512), BF16, "ExternalInput")
    ident_in = dram("ident", (128, 128), BF16, "ExternalInput")

    out = dram("out", (2048, D), F32, "ExternalOutput")

    xkvT = dram("xkvT", (D, S), BF16, scratch_kind)
    xqT = dram("xqT", (D, 2048), BF16, scratch_kind)
    kTs = dram("kTs", (NKV, HD, S), BF16, scratch_kind)
    vSs = dram("vSs", (NKV, KVT, 128, HD), BF16, scratch_kind)
    qTs = dram("qTs", (NSLOT, NH, HD, 128), BF16, scratch_kind)
    ctxTs = dram("ctxTs", (NSLOT, NH, HD, 128), BF16, scratch_kind)
    res2s = dram("res2s", (2048, D), F32, scratch_kind)
    out1s = dram("out1s", (2048, D), F32, scratch_kind)
    xn2Ts = dram("xn2T", (D, 2048), BF16, scratch_kind)
    ffns = dram("ffns", (2048, D), F32, scratch_kind)
    if debug:
        xnD = dram("xnD", (48, 128, D), BF16, "ExternalOutput")
        mvD = dram("mvD", (48, 128, 4), F32, "ExternalOutput")

    with contextlib.ExitStack() as stack:
        sb = stack.enter_context(nc.sbuf_tensor("arena", [128, 206 * 1024], I8))
        ps_f = stack.enter_context(nc.psum_tensor("ps", [128, 4096], F32))
        ps_b = ps_f[:].bitcast(BF16)
        A = Arena(sb[:], 206 * 1024)

        def bankf(i, w=512):
            return ps_f[:, i * 512:i * 512 + w]

        def bankb(i, w=1024):
            return ps_b[:, i * 1024:i * 1024 + w]

        s_ld = Sem(stack, nc, "s_ld")
        s_st = Sem(stack, nc, "s_st")
        s_pe = Sem(stack, nc, "s_pe")
        s_dve = Sem(stack, nc, "s_dve")
        s_act = Sem(stack, nc, "s_act")
        s_dve2 = Sem(stack, nc, "s_dve2")
        s_act2 = Sem(stack, nc, "s_act2")
        s_gp = Sem(stack, nc, "s_gp")
        s_misc = Sem(stack, nc, "s_misc")
        s_x = [Sem(stack, nc, f"s_x{i}") for i in range(2)]
        s_stg = [Sem(stack, nc, f"s_stg{i}") for i in range(2)]
        s_dbg = Sem(stack, nc, "s_dbg")
        s_xn = Sem(stack, nc, "s_xn")
        s_kst = Sem(stack, nc, "s_kst")
        s_vst = Sem(stack, nc, "s_vst")
        s_qst = [Sem(stack, nc, f"s_qst{i}") for i in range(2)]
        s_q2 = [Sem(stack, nc, f"s_q2{i}") for i in range(2)]
        s_kvl = [Sem(stack, nc, f"s_kvl{i}") for i in range(2)]
        s_cst = [Sem(stack, nc, f"s_cst{i}") for i in range(2)]
        dma_sems = [s_ld, s_st, s_misc, s_dbg, s_xn, s_kst, s_vst] + s_x + s_stg + s_qst + s_q2 + s_kvl + s_cst

        W = Waits()

        prog = {e: [] for e in ("sync", "scalar", "tensor", "vector", "act", "gpsimd")}

        all_sems = dma_sems + [s_pe, s_dve, s_act, s_dve2]
        cur_phase = ["P"]

        def emit(en, fn):
            if serialize or (cur_phase[0] in ser_phases):
                vals = tuple((sm, sm.n) for sm in all_sems if sm.n > 0)
                def wrapped(e, vals=vals, fn=fn, key=en):
                    for sm, vv in vals:
                        W.wait(e, key, sm, vv)
                    fn(e)
                prog[en].append(wrapped)
            else:
                prog[en].append(fn)

        def phase_barrier(v=None):
            vals = [(sm, sm.n) for sm in dma_sems if sm.n > 0]
            for en in ("sync", "tensor", "vector", "act", "gpsimd"):
                def w(e, vals=tuple(vals), key=en):
                    for sm, vv in vals:
                        W.wait(e, key, sm, vv)
                emit(en, w)

        ident = A.take((128, 128), BF16)
        emit("sync", lambda e: e.dma_start(out=ident, in_=ident_in).then_inc(s_misc.h, 16))
        v_ident = s_misc.bump(16)

        epsc = A.take((128, 1), F32)
        emit("vector", lambda e: e.memset(epsc, EPS).then_inc(s_dve.h, 1))
        v_eps = s_dve.bump()

        cur_phase[0] = 'A'
        A.push()
        xbuf = [A.take((128, D), F32) for _ in range(2)]
        stg = [A.take((128, 32, 128), BF16) for _ in range(2)]
        bst = A.take((128, 48), F32)
        mv = A.take((128, 2), F32)
        rstd = A.take((128, 1), F32)
        nmr = A.take((128, 1), F32)

        ready_x = [None, None]
        ready_norm = [None, None]
        xn = [A.take((128, D), BF16) for _ in range(2)]
        ready_tr = [None, None]
        ready_ev = [None, None]
        ready_out = [None, None]

        ntiles = KVT + QT_
        for t0 in range(ntiles):
            t = (t0 + QT_) % ntiles
            bi = t0 % 2
            src = xkv if t < KVT else xq
            row0 = (t if t < KVT else t - KVT) * 128
            dstT = xkvT if t < KVT else xqT
            dcol = row0

            def ld(e, src=src, row0=row0, bi=bi, prev=ready_norm[bi]):
                if prev is not None:
                    W.wait(e, "sync", *prev)
                e.dma_start(out=xbuf[bi], in_=src[row0:row0 + 128, :]).then_inc(s_x[bi].h, 16)
            emit("sync", ld)
            ready_x[bi] = (s_x[bi], s_x[bi].bump(16))

            def dve1(e, bi=bi, rx=ready_x[bi]):
                W.wait(e, "vector", *rx)
                for c in range(7):
                    e.bn_stats(bst[:, 6 * c:6 * c + 6], xbuf[bi][:, 512 * c:512 * (c + 1)])
                e.bn_stats(bst[:, 42:48], xbuf[bi][:, 3584:4096]).then_inc(s_dve.h, 1)
            emit("vector", dve1)
            v_bn = s_dve.bump()

            def dve1b(e, v=v_bn):
                W.wait(e, "vector", s_dve, v)
                e.bn_aggr(mv[:], bst[:]).then_inc(s_dve.h, 1)
            emit("vector", dve1b)
            v_stats = s_dve.bump()

            def act1(e, v=v_stats):
                W.wait(e, "act", s_dve, v_eps)
                W.wait(e, "act", s_dve, v)
                e.activation(rstd[:], mv[:, 1:2], AF.Sqrt, bias=epsc).then_inc(s_act.h, 1)
            emit("act", act1)
            v_sqrt = s_act.bump()

            def dve2(e, v=v_sqrt):
                W.wait(e, "vector", s_act, v)
                e.reciprocal(rstd[:], rstd[:]).then_inc(s_dve.h, 1)
            emit("vector", dve2)
            v_rcp = s_dve.bump()

            def dve2b(e, v=v_rcp):
                W.wait(e, "vector", s_dve, v)
                e.scalar_tensor_tensor(nmr[:], mv[:, 0:1], -1.0, rstd[:], ALU.mult, ALU.mult).then_inc(s_dve.h, 1)
            emit("vector", dve2b)
            v_rstd = s_dve.bump()

            def act2(e, bi=bi, v=v_rstd, prev_tr=ready_tr[bi]):
                W.wait(e, "act", s_dve, v)
                if prev_tr is not None:
                    W.wait(e, "act", *prev_tr)
                e.activation(xn[bi][:], xbuf[bi], AF.Identity, bias=nmr[:], scale=rstd[:]).then_inc(s_act.h, 1)
            emit("act", act2)
            ready_norm[bi] = (s_act, s_act.bump())

            def pe1(e, bi=bi, vn=ready_norm[bi], prev_ev=ready_ev[bi]):
                W.wait(e, "tensor", s_misc, v_ident)
                W.wait(e, "tensor", *vn)
                if prev_ev is not None:
                    W.wait(e, "tensor", *prev_ev)
                ins = None
                for c in range(32):
                    ins = e.transpose(bankb(4 * bi)[:, 0:0] if False else
                                      ps_b[:, 4 * bi * 1024 + c * 128: 4 * bi * 1024 + c * 128 + 128],
                                      xn[bi][:, c * 128:(c + 1) * 128], ident)
                ins.then_inc(s_pe.h, 1)
            emit("tensor", pe1)
            ready_tr[bi] = (s_pe, s_pe.bump())

            def dve3(e, bi=bi, vt=ready_tr[bi], prev_out=ready_out[bi]):
                W.wait(e, "vector", *vt)
                if prev_out is not None:
                    W.wait(e, "vector", *prev_out)
                e.tensor_copy(stg[bi].rearrange("p a b -> p (a b)"),
                              ps_b[:, 4 * bi * 1024: 4 * bi * 1024 + 4096]).then_inc(s_dve.h, 1)
            emit("vector", dve3)
            ready_ev[bi] = (s_dve, s_dve.bump())

            def st1(e, bi=bi, dstT=dstT, dcol=dcol, ve=ready_ev[bi]):
                W.wait(e, "gpsimd", *ve)
                e.dma_start(out=dstT.rearrange("(c p) r -> p c r", p=128)[:, :, dcol:dcol + 128],
                            in_=stg[bi]).then_inc(s_stg[bi].h, 16)
            emit("gpsimd", st1)
            ready_out[bi] = (s_stg[bi], s_stg[bi].bump(16))

            if debug:
                def dbg(e, t=t, bi=bi, vn=ready_norm[bi], vt=ready_tr[bi]):
                    W.wait(e, "gpsimd", *vt)
                    e.dma_start(out=xnD[t], in_=xn[bi][:]).then_inc(s_dbg.h, 16)
                    with nc.allow_non_contiguous_dma(reason="debug"):
                        e.dma_start(out=mvD[t, :, 0:2], in_=mv[:]).then_inc(s_dbg.h, 16)
                        e.dma_start(out=mvD[t, :, 2:3], in_=rstd[:]).then_inc(s_dbg.h, 16)
                        e.dma_start(out=mvD[t, :, 3:4], in_=nmr[:]).then_inc(s_dbg.h, 16)
                emit("gpsimd", dbg)
                s_dbg.bump(64)

        v_phaseA_store = None
        A.pop()
        phase_barrier()

        cur_phase[0] = 'B'
        A.push()
        wk_sb = A.take((128, 32, 1024), BF16)
        wv_sb = A.take((128, 32, 1024), BF16)
        cktab = A.take((128, S), BF16)
        sktab = A.take((128, S), BF16)
        qbuf = A.take((128, 1024), I32)
        sbuf_s = A.take((128, 1024), F32)
        xnTb = A.take((128, 32, 256), BF16)
        kstg = A.take((128, 8, 256), BF16)
        vstg = A.take((128, 2, 1024), BF16)
        rswp = A.take((128, 512), F32)
        rt1 = A.take((128, 512), F32)
        rt2 = A.take((128, 512), F32)

        for tin, tout in ((cosk, cktab), (sink, sktab)):
            def ldt(e, tin=tin, tout=tout):
                e.dma_start(out=tout, in_=tin).then_inc(s_misc.h, 16)
            emit("gpsimd", ldt)
            s_misc.bump(16)
        v_tabs = s_misc.n

        for qw_in, sc_in, wsb in ((qwk, sk, wk_sb), (qwv, sv, wv_sb)):
            for dmc in range(32):
                def ldw(e, qw_in=qw_in, sc_in=sc_in, dmc=dmc, vfree=s_dve.n):
                    W.wait(e, "sync", s_dve, vfree)
                    e.dma_start(out=qbuf, in_=qw_in[dmc * 128:(dmc + 1) * 128, :]).then_inc(s_ld.h, 16)
                    e.dma_start(out=sbuf_s, in_=sc_in[dmc * 128:(dmc + 1) * 128, :]).then_inc(s_ld.h, 16)
                emit("sync", ldw)
                vl = s_ld.bump(32)

                def dq(e, wsb=wsb, dmc=dmc, vl=vl):
                    W.wait(e, "vector", s_ld, vl)
                    e.scalar_tensor_tensor(wsb[:, dmc, :], qbuf, -8.0, sbuf_s,
                                           ALU.add, ALU.mult).then_inc(s_dve.h, 1)
                emit("vector", dq)
                s_dve.bump()
        v_wkv = s_dve.n

        NRC = 16
        v_xn_used = 0
        kv_store_v = 0
        for rc in range(16):
            def ldx(e, rc=rc, vp=v_xn_used):
                if vp:
                    W.wait(e, "sync", s_pe, vp)
                e.dma_start(out=xnTb,
                            in_=xkvT.rearrange("(c p) r -> p c r", p=128)[:, :, rc * 256:(rc + 1) * 256]
                            ).then_inc(s_xn.h, 16)
            emit("sync", ldx)
            v_xl = s_xn.bump(16)

            for h8 in range(8):
                bk = h8 % 2
                def pek(e, h8=h8, bk=bk, v_xl=v_xl, vw=v_wkv,
                        v_ev=W.seen.get(("pekfree", bk), 0)):
                    W.wait(e, "tensor", s_xn, v_xl)
                    W.wait(e, "tensor", s_dve, vw)
                    if v_ev:
                        W.wait(e, "tensor", s_dve2, v_ev)
                    for dmc in range(32):
                        ins = e.matmul(bankf(bk, 256), wk_sb[:, dmc, h8 * 128:(h8 + 1) * 128],
                                       xnTb[:, dmc, :], start=(dmc == 0), stop=(dmc == 31))
                    ins.then_inc(s_pe.h, 1)
                emit("tensor", pek)
                v_pe_k = s_pe.bump()
                v_xn_used = v_pe_k

                def ropek(e, h8=h8, bk=bk, rc=rc, v_pe_k=v_pe_k,
                          vks=W.seen.get(("kstgfree",), 0)):
                    W.wait(e, "vector", s_misc, v_tabs)
                    W.wait(e, "vector", s_pe, v_pe_k)
                    if vks:
                        W.wait(e, "vector", s_kst, vks)
                    pb = bankf(bk, 256)
                    e.tensor_copy(rswp[0:64, 0:256], pb[64:128, :])
                    e.tensor_copy(rswp[64:128, 0:256], pb[0:64, :])
                    e.tensor_tensor(rt1[:, 0:256], pb, cktab[:, rc * 256:(rc + 1) * 256], ALU.mult)
                    e.tensor_tensor(rt2[:, 0:256], rswp[:, 0:256], sktab[:, rc * 256:(rc + 1) * 256], ALU.mult)
                    e.tensor_tensor(kstg[:, h8, :], rt1[:, 0:256], rt2[:, 0:256], ALU.add).then_inc(s_dve2.h, 1)
                emit("vector", ropek)
                W.seen[("pekfree", bk)] = s_dve2.bump()

            v_rope_done = s_dve2.n

            def stk(e, rc=rc, v=v_rope_done):
                W.wait(e, "gpsimd", s_dve2, v)
                e.dma_start(out=kTs.rearrange("h p r -> p h r")[:, :, rc * 256:(rc + 1) * 256],
                            in_=kstg).then_inc(s_kst.h, 16)
            emit("gpsimd", stk)
            W.seen[("kstgfree",)] = s_kst.bump(16)

            for rt in range(2):
                def pev(e, rt=rt, v_xl=v_xl, vw=v_wkv,
                        v_ev=W.seen.get(("pevfree", rt), 0)):
                    W.wait(e, "tensor", s_xn, v_xl)
                    W.wait(e, "tensor", s_dve, vw)
                    if v_ev:
                        W.wait(e, "tensor", s_dve2, v_ev)
                    for half in range(2):
                        for dmc in range(32):
                            ins = e.matmul(bankf(2 + 2 * rt + half),
                                           xnTb[:, dmc, rt * 128:(rt + 1) * 128],
                                           wv_sb[:, dmc, half * 512:(half + 1) * 512],
                                           start=(dmc == 0), stop=(dmc == 31))
                    ins.then_inc(s_pe.h, 1)
                emit("tensor", pev)
                v_pe_v = s_pe.bump()
                v_xn_used = v_pe_v

                def evv(e, rt=rt, v_pe_v=v_pe_v, vst=W.seen.get(("vstgfree",), 0)):
                    W.wait(e, "vector", s_pe, v_pe_v)
                    if vst:
                        W.wait(e, "vector", s_vst, vst)
                    e.tensor_copy(vstg[:, rt, :], ps_f[:, (2 + 2 * rt) * 512:(4 + 2 * rt) * 512]).then_inc(s_dve2.h, 1)
                emit("vector", evv)
                W.seen[("pevfree", rt)] = s_dve2.bump()
            v_vev = s_dve2.n

            def stv(e, rc=rc, v=v_vev):
                W.wait(e, "gpsimd", s_dve2, v)
                for rt in range(2):
                    e.dma_start(out=vSs[:, rc * 2 + rt].rearrange("h p d -> p h d"),
                                in_=vstg[:, rt].rearrange("p (h d) -> p h d", h=8)).then_inc(s_vst.h, 16)
            emit("gpsimd", stv)
            W.seen[("vstgfree",)] = s_vst.bump(32)

        v_phaseB_store = None
        A.pop()
        phase_barrier()

        cur_phase[0] = 'C'
        A.push()
        xqf = A.take((128, 32, 2048), BF16)
        cqtab = A.take((128, 2048), BF16)
        sqtab = A.take((128, 2048), BF16)
        wqf = [A.take((128, 32, 128), BF16) for _ in range(2)]
        qwb = [A.take((128, 32, 128), BF16) for _ in range(2)]
        sqb = [A.take((128, 32, 128), BF16) for _ in range(2)]
        qstg = [A.take((128, 16, 128), BF16) for _ in range(2)]
        cswp = A.take((128, 512), F32)
        ct1 = A.take((128, 512), F32)
        ct2 = A.take((128, 512), F32)

        def ldxq(e):
            e.dma_start(out=xqf, in_=xqT.rearrange("(c p) r -> p c r", p=128)).then_inc(s_xn.h, 16)
        emit("sync", ldxq)
        v_xqf = s_xn.bump(16)

        for tin, tout in ((cosq, cqtab), (sinq, sqtab)):
            def ldt2(e, tin=tin, tout=tout):
                e.dma_start(out=tout, in_=tin).then_inc(s_misc.h, 16)
            emit("gpsimd", ldt2)
            s_misc.bump(16)
        v_qtabs = s_misc.n

        for f in range(NH):
            bi = f % 2
            def ldwq(e, f=f, bi=bi, vfree=W.seen.get(("wqdq", bi), 0)):
                if vfree:
                    W.wait(e, "gpsimd", s_dve, vfree)
                e.dma_start(out=qwb[bi],
                            in_=qwq.rearrange("(c p) (f j) -> p c f j", p=128, j=128)[:, :, f, :]
                            ).then_inc(s_ld.h, 16)
                e.dma_start(out=sqb[bi],
                            in_=sq.rearrange("(c p) (f j) -> p c f j", p=128, j=128)[:, :, f, :]
                            ).then_inc(s_ld.h, 16)
            emit("gpsimd", ldwq)
            v_lw = s_ld.bump(32)

            def dqq(e, bi=bi, v_lw=v_lw, vused=W.seen.get(("wqused", bi), 0)):
                W.wait(e, "vector", s_ld, v_lw)
                if vused:
                    W.wait(e, "vector", s_pe, vused)
                e.scalar_tensor_tensor(wqf[bi].rearrange("p a b -> p (a b)"),
                                       qwb[bi].rearrange("p a b -> p (a b)"),
                                       -8.0, sqb[bi].rearrange("p a b -> p (a b)"),
                                       ALU.add, ALU.mult).then_inc(s_dve.h, 1)
            emit("vector", dqq)
            v_dq = s_dve.bump()
            W.seen[("wqdq", bi)] = v_dq

            for qc in range(4):
                bk = qc % 2
                def peq(e, f=f, bi=bi, qc=qc, bk=bk, v_dq=v_dq,
                        vev=W.seen.get(("peqfree", bk), 0)):
                    W.wait(e, "tensor", s_xn, v_xqf)
                    W.wait(e, "tensor", s_dve, v_dq)
                    if vev:
                        W.wait(e, "tensor", s_dve2, vev)
                    for dmc in range(32):
                        ins = e.matmul(bankf(bk), wqf[bi][:, dmc, :],
                                       xqf[:, dmc, qc * 512:(qc + 1) * 512],
                                       start=(dmc == 0), stop=(dmc == 31))
                    ins.then_inc(s_pe.h, 1)
                emit("tensor", peq)
                v_pq = s_pe.bump()
                if qc == 3:
                    W.seen[("wqused", bi)] = v_pq

                def ropeq(e, qc=qc, bk=bk, bi=bi, v_pq=v_pq,
                          vst=W.seen.get(("qstgfree", bi), 0)):
                    W.wait(e, "vector", s_misc, v_qtabs)
                    W.wait(e, "vector", s_pe, v_pq)
                    if vst:
                        W.wait(e, "vector", s_qst[bi], vst)
                    pb = bankf(bk)
                    e.tensor_copy(cswp[0:64, :], pb[64:128, :])
                    e.tensor_copy(cswp[64:128, :], pb[0:64, :])
                    e.tensor_tensor(ct1, pb, cqtab[:, qc * 512:(qc + 1) * 512], ALU.mult)
                    e.tensor_tensor(ct2, cswp, sqtab[:, qc * 512:(qc + 1) * 512], ALU.mult)
                    e.tensor_tensor(qstg[bi].rearrange("p a b -> p (a b)")[:, qc * 512:(qc + 1) * 512],
                                    ct1, ct2, ALU.add).then_inc(s_dve2.h, 1)
                emit("vector", ropeq)
                W.seen[("peqfree", bk)] = s_dve2.bump()
            v_qrope = s_dve2.n

            def stq(e, f=f, bi=bi, v=v_qrope):
                W.wait(e, "gpsimd", s_dve2, v)
                e.dma_start(out=qTs[:, f].rearrange("s p q -> p s q"),
                            in_=qstg[bi]).then_inc(s_qst[bi].h, 16)
            emit("gpsimd", stq)
            W.seen[("qstgfree", bi)] = s_qst[bi].bump(16)

        v_phaseC_store = None
        A.pop()
        phase_barrier()

        cur_phase[0] = 'D'
        A.push()
        qslot = [A.take((128, NH, 128), BF16) for _ in range(2)]
        kbuf = [A.take((128, S), BF16) for _ in range(2)]
        vbuf = [A.take((128, KVT, 128), BF16) for _ in range(2)]
        sP = [A.take((128, S), F32) for _ in range(2)]
        Pex = [A.take((128, S), BF16) for _ in range(2)]
        mxcols = [A.take((128, 8), F32) for _ in range(2)]
        negm = [A.take((128, 1), F32) for _ in range(2)]
        ssum = [A.take((128, 1), F32) for _ in range(2)]
        rsc = [A.take((128, 1), F32) for _ in range(2)]
        PT = A.take((128, KVT, 128), BF16)
        ctxstage = [A.take((128, NH, 128), BF16) for _ in range(2)]
        dm_sb = A.take((128, NSLOT, 512), BF16)
        zer512 = A.take((128, 512), BF16)

        def ldmask(e):
            e.dma_start(out=dm_sb, in_=dmask.rearrange("s p c -> p s c")).then_inc(s_misc.h, 16)
        emit("sync", ldmask)
        v_mask = s_misc.bump(16)
        emit("vector", lambda e: e.memset(zer512, 0.0).then_inc(s_dve.h, 1))
        v_zer = s_dve.bump()

        def stage1(j, h, jp, kp, v_qs, v_kv):
            E = _ext(j)
            nk = (E + 511) // 512
            hp = h % 2
            sb0 = 2 * hp
            for c in range(nk):
                c0 = c * 512
                w = min(512, E - c0)
                bk = sb0 + c % 2
                def sc(e, c0=c0, w=w, bk=bk, v_qs=v_qs, v_kv=v_kv, h=h,
                       jp=jp, kp=kp, vfree=W.seen.get(("scfree", bk), 0)):
                    W.wait(e, "tensor", s_q2[jp], v_qs)
                    W.wait(e, "tensor", s_kvl[kp], v_kv)
                    if vfree:
                        W.wait(e, "tensor", s_dve, vfree)
                    e.matmul(bankf(bk, w), qslot[jp][:, h, :], kbuf[kp][:, c0:c0 + w],
                             start=True, stop=True).then_inc(s_pe.h, 1)
                emit("tensor", sc)
                v_sc = s_pe.bump()
                islast = (c == nk - 1)

                def ttr(e, c=c, c0=c0, w=w, bk=bk, hp=hp, j=j, islast=islast, v_sc=v_sc):
                    W.wait(e, "vector", s_misc, v_mask)
                    W.wait(e, "vector", s_dve, v_zer)
                    W.wait(e, "vector", s_pe, v_sc)
                    m = dm_sb[:, j, 512 - w:512] if islast else zer512[:, 0:w]
                    e.scalar_tensor_tensor(sP[hp][:, c0:c0 + w], bankf(bk, w), ISQ, m,
                                           ALU.mult, ALU.add)
                    e.reduce_max(mxcols[hp][:, c:c + 1], sP[hp][:, c0:c0 + w],
                                 AX.X).then_inc(s_dve.h, 1)
                emit("vector", ttr)
                va = s_dve.bump()
                W.seen[("scfree", bk)] = va
            v_ttr_last = s_dve.n

            def mxred(e, hp=hp, nk=nk, v=v_ttr_last):
                W.wait(e, "vector", s_dve, v)
                e.reduce_max(negm[hp], mxcols[hp][:, 0:nk], AX.X, negate=True).then_inc(s_dve.h, 1)
            emit("vector", mxred)
            v_max = s_dve.bump()

            def expop(e, E=E, hp=hp, v_max=v_max, vtr=W.seen.get(("trdone", hp), 0)):
                W.wait(e, "act", s_dve, v_max)
                if vtr:
                    W.wait(e, "act", s_pe, vtr)
                e.activation(Pex[hp][:, 0:E], sP[hp][:, 0:E], AF.Exp, bias=negm[hp],
                             accum_out=ssum[hp]).then_inc(s_act.h, 1)
            emit("act", expop)
            v_exp = s_act.bump()

            def rcp(e, hp=hp, v=v_exp):
                W.wait(e, "vector", s_act, v)
                e.reciprocal(rsc[hp], ssum[hp]).then_inc(s_dve.h, 1)
            emit("vector", rcp)
            v_rcp2 = s_dve.bump()

            def nrm(e, E=E, hp=hp, v=v_rcp2):
                W.wait(e, "gpsimd", s_dve, v)
                e.tensor_scalar(Pex[hp][:, 0:E], Pex[hp][:, 0:E], rsc[hp], None,
                                ALU.mult).then_inc(s_gp.h, 1)
            emit("gpsimd", nrm)
            return s_gp.bump()

        def stage2(j, h, jp, kp, v_nrm):
            E = _ext(j)
            nt = E // 128
            hp = h % 2
            ngr = (nt + 7) // 8
            v_ctx_prev = W.seen.get(("ctxdone",), 0)
            trev = []
            v_tr = None
            for g in range(ngr):
                t0g, t1g = 8 * g, min(8 * g + 8, nt)
                bk = 4 + g % 2
                def trp(e, t0g=t0g, t1g=t1g, bk=bk, hp=hp, v_nrm=v_nrm,
                        vfree=W.seen.get(("trfree", bk), 0)):
                    W.wait(e, "tensor", s_misc, v_ident)
                    W.wait(e, "tensor", s_gp, v_nrm)
                    if vfree:
                        W.wait(e, "tensor", s_act2, vfree)
                    for kt in range(t0g, t1g):
                        ins = e.transpose(bankb(bk)[:, (kt - t0g) * 128:(kt - t0g + 1) * 128],
                                          Pex[hp][:, kt * 128:(kt + 1) * 128], ident)
                    ins.then_inc(s_pe.h, 1)
                emit("tensor", trp)
                v_tr = s_pe.bump()

                def trev_f(e, t0g=t0g, t1g=t1g, bk=bk, v_tr=v_tr, vctx=v_ctx_prev):
                    W.wait(e, "act", s_pe, v_tr)
                    if vctx:
                        W.wait(e, "act", s_pe, vctx)
                    e.activation(PT[:, t0g:t1g, :].rearrange("p a b -> p (a b)"),
                                 bankb(bk)[:, 0:(t1g - t0g) * 128],
                                 AF.Identity).then_inc(s_act2.h, 1)
                emit("act", trev_f)
                vv = s_act2.bump()
                trev.append(vv)
                W.seen[("trfree", bk)] = vv
            W.seen[("trdone", hp)] = v_tr

            def ctxop(e, nt=nt, kp=kp, vtr=trev[-1],
                      vcxfree=W.seen.get(("cxfree",), 0)):
                W.wait(e, "tensor", s_act2, vtr)
                if vcxfree:
                    W.wait(e, "tensor", s_act2, vcxfree)
                for kt in range(nt):
                    ins = e.matmul(bankf(6, 128), vbuf[kp][:, kt, :], PT[:, kt, :],
                                   start=(kt == 0), stop=(kt == nt - 1))
                ins.then_inc(s_pe.h, 1)
            emit("tensor", ctxop)
            v_ctx = s_pe.bump()
            W.seen[("ctxdone",)] = v_ctx
            if h % 4 == 3:
                W.seen[("kvfree", kp)] = v_ctx
            if h == NH - 1:
                W.seen[("qslotfree", jp)] = v_ctx

            def cxev(e, h=h, jp=jp, v=v_ctx,
                     vstg=W.seen.get(("cstgfree", jp), 0)):
                W.wait(e, "act", s_pe, v)
                if vstg:
                    W.wait(e, "act", s_cst[jp], vstg)
                e.activation(ctxstage[jp][:, h, :], bankf(6, 128),
                             AF.Identity).then_inc(s_act2.h, 1)
            emit("act", cxev)
            vce = s_act2.bump()
            W.seen[("cxfree",)] = vce
            if h == NH - 1:
                def stctx(e, j=j, jp=jp, v=s_act2.n):
                    W.wait(e, "gpsimd", s_act2, v)
                    e.dma_start(out=ctxTs[j].rearrange("h d q -> d h q"),
                                in_=ctxstage[jp]).then_inc(s_cst[jp].h, 16)
                emit("gpsimd", stctx)
                W.seen[("cstgfree", jp)] = s_cst[jp].bump(16)

        pending = None
        for j in range(NSLOT):
            E = _ext(j)
            nt = E // 128
            jp = j % 2

            def ldq(e, j=j, jp=jp, vfree=W.seen.get(("qslotfree", j % 2), 0)):
                if vfree:
                    W.wait(e, "sync", s_pe, vfree)
                e.dma_start(out=qslot[jp], in_=qTs[j].rearrange("h p q -> p h q")).then_inc(s_q2[jp].h, 16)
            emit("sync", ldq)
            v_qs = s_q2[jp].bump(16)

            for kvh in range(NKV):
                kp = (j * NKV + kvh) % 2
                def ldkv(e, kvh=kvh, kp=kp, E=E, nt=nt,
                         vfree=W.seen.get(("kvfree", kp), 0)):
                    if vfree:
                        W.wait(e, "sync", s_pe, vfree)
                    e.dma_start(out=kbuf[kp][:, 0:E], in_=kTs[kvh][:, 0:E]).then_inc(s_kvl[kp].h, 16)
                    e.dma_start(out=vbuf[kp][:, 0:nt, :],
                                in_=vSs[kvh, 0:nt].rearrange("t p d -> p t d")).then_inc(s_kvl[kp].h, 16)
                emit("sync", ldkv)
                v_kv = s_kvl[kp].bump(32)

                for hh in range(4):
                    h = kvh * 4 + hh
                    v_nrm = stage1(j, h, jp, kp, v_qs, v_kv)
                    if pending is not None:
                        stage2(*pending)
                    pending = (j, h, jp, kp, v_nrm)
        stage2(*pending)
        pending = None

        v_phaseD_store = None
        A.pop()
        phase_barrier()

        cur_phase[0] = 'O1'
        A.push()
        ctxf = A.take((128, NH, 2048), BF16)
        qob = A.take((128, 32, 256), BF16)
        sob = A.take((128, 32, 256), BF16)
        wof = A.take((128, 32, 256), BF16)
        o1stg = [A.take((128, 256), F32) for _ in range(2)]

        for sl in range(NSLOT):
            def ldcx(e, sl=sl):
                e.dma_start(out=ctxf[:, :, sl * 128:(sl + 1) * 128],
                            in_=ctxTs[sl].rearrange("h d q -> d h q")).then_inc(s_xn.h, 16)
            emit("sync", ldcx)
        v_ctxf = s_xn.bump(16 * NSLOT)

        for jc in range(16):
            def ldwo(e, jc=jc, vfree=W.seen.get(("wodq",), 0)):
                if vfree:
                    W.wait(e, "gpsimd", s_dve, vfree)
                e.dma_start(out=qob,
                            in_=qwo.rearrange("(c p) (f j) -> p c f j", p=128, j=256)[:, :, jc, :]
                            ).then_inc(s_ld.h, 16)
                e.dma_start(out=sob,
                            in_=so.rearrange("(c p) (f j) -> p c f j", p=128, j=256)[:, :, jc, :]
                            ).then_inc(s_ld.h, 16)
            emit("gpsimd", ldwo)
            v_lwo = s_ld.bump(32)

            def dqo(e, v_lwo=v_lwo, vused=W.seen.get(("woused",), 0)):
                W.wait(e, "vector", s_ld, v_lwo)
                if vused:
                    W.wait(e, "vector", s_pe, vused)
                e.scalar_tensor_tensor(wof.rearrange("p a b -> p (a b)"),
                                       qob.rearrange("p a b -> p (a b)"), -8.0,
                                       sob.rearrange("p a b -> p (a b)"),
                                       ALU.add, ALU.mult).then_inc(s_dve.h, 1)
            emit("vector", dqo)
            v_dqo = s_dve.bump()
            W.seen[("wodq",)] = v_dqo

            for qt in range(NSLOT):
                bk = qt % 2
                def peo(e, jc=jc, qt=qt, bk=bk, v_dqo=v_dqo,
                        vev=W.seen.get(("peofree", bk), 0)):
                    W.wait(e, "tensor", s_xn, v_ctxf)
                    W.wait(e, "tensor", s_dve, v_dqo)
                    if vev:
                        W.wait(e, "tensor", s_dve2, vev)
                    for dmc in range(32):
                        ins = e.matmul(bankf(bk, 256), ctxf[:, dmc, qt * 128:(qt + 1) * 128],
                                       wof[:, dmc, :], start=(dmc == 0), stop=(dmc == 31))
                    ins.then_inc(s_pe.h, 1)
                emit("tensor", peo)
                v_po = s_pe.bump()
                if qt == NSLOT - 1:
                    W.seen[("woused",)] = v_po

                def evo(e, bk=bk, qt=qt, v_po=v_po,
                        vst=W.seen.get(("o1free", qt % 2), 0)):
                    W.wait(e, "vector", s_pe, v_po)
                    if vst:
                        W.wait(e, "vector", s_stg[qt % 2], vst)
                    e.tensor_copy(o1stg[qt % 2], bankf(bk, 256)).then_inc(s_dve2.h, 1)
                emit("vector", evo)
                vevo = s_dve2.bump()
                W.seen[("peofree", bk)] = vevo

                def sto(e, jc=jc, qt=qt, v=vevo):
                    W.wait(e, "gpsimd", s_dve2, v)
                    e.dma_start(out=out1s[qt * 128:(qt + 1) * 128, jc * 256:(jc + 1) * 256],
                                in_=o1stg[qt % 2]).then_inc(s_stg[qt % 2].h, 16)
                emit("gpsimd", sto)
                W.seen[("o1free", qt % 2)] = s_stg[qt % 2].bump(16)

        A.pop()
        phase_barrier()

        cur_phase[0] = 'O2'
        A.push()
        obuf = [A.take((128, D), F32) for _ in range(2)]
        xqb = [A.take((128, D), F32) for _ in range(2)]
        stg2 = [A.take((128, 32, 128), BF16) for _ in range(2)]
        r2b = [A.take((128, D), F32) for _ in range(2)]
        xn2b = [A.take((128, D), BF16) for _ in range(2)]
        bst2 = A.take((128, 48), F32)
        mv2 = A.take((128, 2), F32)
        rstd2 = A.take((128, 1), F32)
        nmr2 = A.take((128, 1), F32)
        ln1b = A.take((128, D), F32)

        ready_n2 = [None, None]
        ready_tr2 = [None, None]
        ready_ev2 = [None, None]
        ready_out2 = [None, None]
        ready_r2st = [None, None]

        for t in range(QT_):
            bi = t % 2

            def ld2(e, t=t, bi=bi, prev=ready_n2[bi]):
                if prev is not None:
                    W.wait(e, "sync", *prev)
                e.dma_start(out=obuf[bi], in_=out1s[t * 128:(t + 1) * 128, :]).then_inc(s_x[bi].h, 16)
                e.dma_start(out=xqb[bi], in_=xq[t * 128:(t + 1) * 128, :]).then_inc(s_x[bi].h, 16)
            emit("sync", ld2)
            rx = (s_x[bi], s_x[bi].bump(32))

            def st_a(e, bi=bi, rx=rx):
                W.wait(e, "vector", *rx)
                for c in range(7):
                    e.bn_stats(bst2[:, 6 * c:6 * c + 6], obuf[bi][:, 512 * c:512 * (c + 1)])
                e.bn_stats(bst2[:, 42:48], obuf[bi][:, 3584:4096]).then_inc(s_dve.h, 1)
            emit("vector", st_a)
            v_s0 = s_dve.bump()

            def st_a2(e, v=v_s0):
                W.wait(e, "vector", s_dve, v)
                e.bn_aggr(mv2[:], bst2[:]).then_inc(s_dve.h, 1)
            emit("vector", st_a2)
            v_s1 = s_dve.bump()

            def st_b(e, v=v_s1):
                W.wait(e, "act", s_dve, v)
                e.activation(rstd2[:], mv2[:, 1:2], AF.Sqrt, bias=epsc).then_inc(s_act.h, 1)
            emit("act", st_b)
            v_s2 = s_act.bump()

            def st_c(e, v=v_s2):
                W.wait(e, "vector", s_act, v)
                e.reciprocal(rstd2[:], rstd2[:]).then_inc(s_dve.h, 1)
            emit("vector", st_c)
            v_s3 = s_dve.bump()

            def st_d(e, v=v_s3):
                W.wait(e, "vector", s_dve, v)
                e.scalar_tensor_tensor(nmr2[:], mv2[:, 0:1], -1.0, rstd2[:], ALU.mult,
                                       ALU.mult).then_inc(s_dve.h, 1)
            emit("vector", st_d)
            v_s4 = s_dve.bump()

            def st_e(e, bi=bi, v=v_s4, prevadd=W.seen.get(("ln1free",), 0)):
                W.wait(e, "act", s_dve, v)
                if prevadd:
                    W.wait(e, "act", s_dve, prevadd)
                e.activation(ln1b[:], obuf[bi], AF.Identity, bias=nmr2[:],
                             scale=rstd2[:]).then_inc(s_act.h, 1)
            emit("act", st_e)
            v_ln1 = s_act.bump()

            def st_g(e, bi=bi, v=v_ln1, prevst=ready_r2st[bi], prevn2=ready_n2[bi]):
                W.wait(e, "vector", s_act, v)
                if prevst is not None:
                    W.wait(e, "vector", *prevst)
                e.tensor_tensor(r2b[bi], xqb[bi], ln1b[:], ALU.add).then_inc(s_dve.h, 1)
            emit("vector", st_g)
            v_r2 = s_dve.bump()
            W.seen[("ln1free",)] = v_r2

            def st_h(e, bi=bi, t=t, v=v_r2):
                W.wait(e, "gpsimd", s_dve, v)
                e.dma_start(out=res2s[t * 128:(t + 1) * 128, :], in_=r2b[bi]).then_inc(s_qst[bi].h, 16)
            emit("gpsimd", st_h)
            ready_r2st[bi] = (s_qst[bi], s_qst[bi].bump(16))

            def st_i(e, bi=bi, v=v_r2):
                W.wait(e, "vector", s_dve, v)
                for c in range(7):
                    e.bn_stats(bst2[:, 6 * c:6 * c + 6], r2b[bi][:, 512 * c:512 * (c + 1)])
                e.bn_stats(bst2[:, 42:48], r2b[bi][:, 3584:4096]).then_inc(s_dve.h, 1)
            emit("vector", st_i)
            v_t0 = s_dve.bump()

            def st_i2(e, v=v_t0):
                W.wait(e, "vector", s_dve, v)
                e.bn_aggr(mv2[:], bst2[:]).then_inc(s_dve.h, 1)
            emit("vector", st_i2)
            v_t1 = s_dve.bump()

            def st_j(e, v=v_t1):
                W.wait(e, "act", s_dve, v)
                e.activation(rstd2[:], mv2[:, 1:2], AF.Sqrt, bias=epsc).then_inc(s_act.h, 1)
            emit("act", st_j)
            v_t2 = s_act.bump()

            def st_k(e, v=v_t2):
                W.wait(e, "vector", s_act, v)
                e.reciprocal(rstd2[:], rstd2[:]).then_inc(s_dve.h, 1)
            emit("vector", st_k)
            v_t3 = s_dve.bump()

            def st_l(e, v=v_t3):
                W.wait(e, "vector", s_dve, v)
                e.scalar_tensor_tensor(nmr2[:], mv2[:, 0:1], -1.0, rstd2[:], ALU.mult,
                                       ALU.mult).then_inc(s_dve.h, 1)
            emit("vector", st_l)
            v_t4 = s_dve.bump()

            def st_m(e, bi=bi, v=v_t4, prev_tr=ready_tr2[bi]):
                W.wait(e, "act", s_dve, v)
                if prev_tr is not None:
                    W.wait(e, "act", *prev_tr)
                e.activation(xn2b[bi][:], r2b[bi], AF.Identity, bias=nmr2[:],
                             scale=rstd2[:]).then_inc(s_act.h, 1)
            emit("act", st_m)
            ready_n2[bi] = (s_act, s_act.bump())

            def st_n(e, bi=bi, vn=ready_n2[bi], prev_ev=ready_ev2[bi]):
                W.wait(e, "tensor", s_misc, v_ident)
                W.wait(e, "tensor", *vn)
                if prev_ev is not None:
                    W.wait(e, "tensor", *prev_ev)
                for c in range(32):
                    ins = e.transpose(ps_b[:, 4 * bi * 1024 + c * 128: 4 * bi * 1024 + c * 128 + 128],
                                      xn2b[bi][:, c * 128:(c + 1) * 128], ident)
                ins.then_inc(s_pe.h, 1)
            emit("tensor", st_n)
            ready_tr2[bi] = (s_pe, s_pe.bump())

            def st_o(e, bi=bi, vt=ready_tr2[bi], prev_out=ready_out2[bi]):
                W.wait(e, "vector", *vt)
                if prev_out is not None:
                    W.wait(e, "vector", *prev_out)
                e.tensor_copy(stg2[bi].rearrange("p a b -> p (a b)"),
                              ps_b[:, 4 * bi * 1024: 4 * bi * 1024 + 4096]).then_inc(s_dve.h, 1)
            emit("vector", st_o)
            ready_ev2[bi] = (s_dve, s_dve.bump())

            def st_p(e, bi=bi, t=t, ve=ready_ev2[bi]):
                W.wait(e, "gpsimd", *ve)
                e.dma_start(out=xn2Ts.rearrange("(c p) r -> p c r", p=128)[:, :, t * 128:(t + 1) * 128],
                            in_=stg2[bi]).then_inc(s_stg[bi].h, 16)
            emit("gpsimd", st_p)
            ready_out2[bi] = (s_stg[bi], s_stg[bi].bump(16))

        A.pop()
        phase_barrier()

        cur_phase[0] = 'E'
        A.push()
        xn2f = A.take((128, 32, 2048), BF16)
        fb_sb = A.take((128, D), F32)
        wtp = A.take((128, 32, 512), BF16)
        byts = A.take((128, 1536), I8)
        nrm_sb = A.take((128, 128), F32)
        vals = A.take((128, 4096), I8)
        t1b = A.take((128, 4096), BF16)
        tmp8 = A.take((128, 512), I8)
        festg = [A.take((128, 512), F32) for _ in range(2)]

        def ldxn2(e):
            e.dma_start(out=xn2f, in_=xn2Ts.rearrange("(c p) r -> p c r", p=128)).then_inc(s_xn.h, 16)
            e.dma_start(out=fb_sb, in_=fbias).then_inc(s_xn.h, 16)
        emit("sync", ldxn2)
        v_xn2f = s_xn.bump(32)

        fq3r = fq3.rearrange("(o g) b -> o (g b)", g=128)
        for oc in range(8):
            for ot4 in range(4):
                ot = oc * 4 + ot4
                def ldby(e, ot=ot, vfree=W.seen.get(("unpk",), 0)):
                    if vfree:
                        W.wait(e, "sync", s_dve, vfree)
                    e.dma_start(out=byts, in_=fq3r[ot * 128:(ot + 1) * 128, :]).then_inc(s_ld.h, 16)
                    e.dma_start(out=nrm_sb, in_=fnorm[ot * 128:(ot + 1) * 128, :]).then_inc(s_ld.h, 16)
                emit("sync", ldby)
                v_by = s_ld.bump(32)

                def unpk(e, v_by=v_by, vtr=W.seen.get(("wtrdone",), 0)):
                    W.wait(e, "vector", s_ld, v_by)
                    if vtr:
                        W.wait(e, "vector", s_pe, vtr)
                    b3 = byts.rearrange("p (c three) -> p c three", three=3)
                    v8 = vals.rearrange("p (c eight) -> p c eight", eight=8)
                    b0, b1, b2 = b3[:, :, 0], b3[:, :, 1], b3[:, :, 2]
                    e.tensor_scalar(v8[:, :, 0], b0, 7, None, ALU.bitwise_and)
                    e.tensor_scalar(v8[:, :, 1], b0, 3, 7, ALU.logical_shift_right, ALU.bitwise_and)
                    e.tensor_scalar(tmp8, b1, 1, 2, ALU.bitwise_and, ALU.logical_shift_left)
                    e.tensor_scalar(v8[:, :, 2], b0, 6, 3, ALU.logical_shift_right, ALU.bitwise_and)
                    e.tensor_tensor(v8[:, :, 2], v8[:, :, 2], tmp8, ALU.bitwise_or)
                    e.tensor_scalar(v8[:, :, 3], b1, 1, 7, ALU.logical_shift_right, ALU.bitwise_and)
                    e.tensor_scalar(v8[:, :, 4], b1, 4, 7, ALU.logical_shift_right, ALU.bitwise_and)
                    e.tensor_scalar(tmp8, b2, 3, 1, ALU.bitwise_and, ALU.logical_shift_left)
                    e.tensor_scalar(v8[:, :, 5], b1, 7, 1, ALU.logical_shift_right, ALU.bitwise_and)
                    e.tensor_tensor(v8[:, :, 5], v8[:, :, 5], tmp8, ALU.bitwise_or)
                    e.tensor_scalar(v8[:, :, 6], b2, 2, 7, ALU.logical_shift_right, ALU.bitwise_and)
                    e.tensor_scalar(v8[:, :, 7], b2, 5, 7, ALU.logical_shift_right, ALU.bitwise_and)
                    e.tensor_scalar(t1b, vals, 2.0 / 7.0, -1.0, ALU.mult, ALU.add)
                    wv = t1b.rearrange("p (g k) -> p g k", k=32)
                    nb = nrm_sb.unsqueeze(2).broadcast_to((128, 128, 32))
                    e.tensor_tensor(wv, wv, nb, ALU.mult).then_inc(s_dve.h, 1)
                emit("vector", unpk)
                v_un = s_dve.bump()
                W.seen[("unpk",)] = v_un

                def wtr(e, ot4=ot4, v_un=v_un, vev=W.seen.get(("wtevd",), 0),
                        vmm=W.seen.get(("ffnmmdone",), 0)):
                    W.wait(e, "tensor", s_misc, v_ident)
                    W.wait(e, "tensor", s_dve, v_un)
                    if vev:
                        W.wait(e, "tensor", s_dve2, vev)
                    if ot4 == 0 and vmm:
                        W.wait(e, "tensor", s_pe, vmm)
                    for c in range(32):
                        ins = e.transpose(ps_b[:, (c // 8) * 1024 + (c % 8) * 128:
                                               (c // 8) * 1024 + (c % 8) * 128 + 128],
                                          t1b[:, c * 128:(c + 1) * 128], ident)
                    ins.then_inc(s_pe.h, 1)
                emit("tensor", wtr)
                v_wtr = s_pe.bump()
                W.seen[("wtrdone",)] = v_wtr

                def wtev(e, ot4=ot4, v_wtr=v_wtr, vmm=W.seen.get(("ffnmmdone",), 0)):
                    W.wait(e, "vector", s_pe, v_wtr)
                    if vmm:
                        W.wait(e, "vector", s_pe, vmm)
                    ins = None
                    for b in range(4):
                        ins = e.tensor_copy(wtp[:, 8 * b:8 * b + 8, ot4 * 128:(ot4 + 1) * 128],
                                            ps_b[:, b * 1024:(b + 1) * 1024].rearrange(
                                                "p (a q) -> p a q", a=8))
                    ins.then_inc(s_dve2.h, 1)
                emit("vector", wtev)
                s_dve2.bump()
                W.seen[("wtevd",)] = s_dve2.n
            v_panel = s_dve2.n

            for tt in range(QT_):
                bk = 4 + tt % 2
                def pef(e, oc=oc, tt=tt, bk=bk, v_panel=v_panel,
                        vev=W.seen.get(("fefree", bk), 0)):
                    W.wait(e, "tensor", s_xn, v_xn2f)
                    W.wait(e, "tensor", s_dve2, v_panel)
                    if vev:
                        W.wait(e, "tensor", s_dve2, vev)
                    for ic in range(32):
                        ins = e.matmul(bankf(bk), xn2f[:, ic, tt * 128:(tt + 1) * 128],
                                       wtp[:, ic, :], start=(ic == 0), stop=(ic == 31))
                    ins.then_inc(s_pe.h, 1)
                emit("tensor", pef)
                v_pf = s_pe.bump()
                if tt == QT_ - 1:
                    W.seen[("ffnmmdone",)] = v_pf

                def evf(e, oc=oc, tt=tt, bk=bk, v_pf=v_pf,
                        vst=W.seen.get(("festgfree", tt % 2), 0)):
                    W.wait(e, "vector", s_xn, v_xn2f)
                    W.wait(e, "vector", s_pe, v_pf)
                    if vst:
                        W.wait(e, "vector", s_vst if tt % 2 else s_kst, vst)
                    e.tensor_tensor(festg[tt % 2], bankf(bk), fb_sb[:, oc * 512:(oc + 1) * 512],
                                    ALU.add).then_inc(s_dve2.h, 1)
                emit("vector", evf)
                v_ef = s_dve2.bump()
                W.seen[("fefree", bk)] = v_ef

                def stf(e, oc=oc, tt=tt, v=v_ef):
                    W.wait(e, "gpsimd", s_dve2, v)
                    sm = s_vst if tt % 2 else s_kst
                    e.dma_start(out=ffns[tt * 128:(tt + 1) * 128, oc * 512:(oc + 1) * 512],
                                in_=festg[tt % 2]).then_inc(sm.h, 16)
                emit("gpsimd", stf)
                sm = s_vst if tt % 2 else s_kst
                W.seen[("festgfree", tt % 2)] = sm.bump(16)

        A.pop()
        phase_barrier()

        cur_phase[0] = 'F'
        A.push()
        fbuf = [A.take((128, D), F32) for _ in range(2)]
        r2f = [A.take((128, D), F32) for _ in range(2)]
        obuf2 = [A.take((128, D), F32) for _ in range(2)]
        bst3 = A.take((128, 48), F32)
        mv3 = A.take((128, 2), F32)
        rstd3 = A.take((128, 1), F32)
        nmr3 = A.take((128, 1), F32)
        lnf = A.take((128, D), F32)

        ready_done = [None, None]
        for t in range(QT_):
            bi = t % 2

            def fl(e, t=t, bi=bi, prev=ready_done[bi]):
                if prev is not None:
                    W.wait(e, "sync", *prev)
                e.dma_start(out=fbuf[bi], in_=ffns[t * 128:(t + 1) * 128, :]).then_inc(s_x[bi].h, 16)
                e.dma_start(out=r2f[bi], in_=res2s[t * 128:(t + 1) * 128, :]).then_inc(s_x[bi].h, 16)
            emit("sync", fl)
            rx = (s_x[bi], s_x[bi].bump(32))

            def f_a(e, bi=bi, rx=rx):
                W.wait(e, "vector", *rx)
                for c in range(7):
                    e.bn_stats(bst3[:, 6 * c:6 * c + 6], fbuf[bi][:, 512 * c:512 * (c + 1)])
                e.bn_stats(bst3[:, 42:48], fbuf[bi][:, 3584:4096]).then_inc(s_dve.h, 1)
            emit("vector", f_a)
            v_f0 = s_dve.bump()

            def f_a2(e, v=v_f0):
                W.wait(e, "vector", s_dve, v)
                e.bn_aggr(mv3[:], bst3[:]).then_inc(s_dve.h, 1)
            emit("vector", f_a2)
            v_f1 = s_dve.bump()

            def f_b(e, v=v_f1):
                W.wait(e, "act", s_dve, v)
                e.activation(rstd3[:], mv3[:, 1:2], AF.Sqrt, bias=epsc).then_inc(s_act.h, 1)
            emit("act", f_b)
            v_f2 = s_act.bump()

            def f_c(e, v=v_f2):
                W.wait(e, "vector", s_act, v)
                e.reciprocal(rstd3[:], rstd3[:]).then_inc(s_dve.h, 1)
            emit("vector", f_c)
            v_f3 = s_dve.bump()

            def f_d(e, v=v_f3):
                W.wait(e, "vector", s_dve, v)
                e.scalar_tensor_tensor(nmr3[:], mv3[:, 0:1], -1.0, rstd3[:], ALU.mult,
                                       ALU.mult).then_inc(s_dve.h, 1)
            emit("vector", f_d)
            v_f4 = s_dve.bump()

            def f_e(e, bi=bi, v=v_f4, prevadd=W.seen.get(("lnffree",), 0)):
                W.wait(e, "act", s_dve, v)
                if prevadd:
                    W.wait(e, "act", s_dve, prevadd)
                e.activation(lnf[:], fbuf[bi], AF.Identity, bias=nmr3[:],
                             scale=rstd3[:]).then_inc(s_act.h, 1)
            emit("act", f_e)
            v_lnf = s_act.bump()

            def f_g(e, bi=bi, v=v_lnf, prevst=ready_done[bi]):
                W.wait(e, "vector", s_act, v)
                if prevst is not None:
                    W.wait(e, "vector", *prevst)
                e.tensor_tensor(obuf2[bi], r2f[bi], lnf[:], ALU.add).then_inc(s_dve.h, 1)
            emit("vector", f_g)
            v_o = s_dve.bump()
            W.seen[("lnffree",)] = v_o

            def f_h(e, bi=bi, t=t, v=v_o):
                W.wait(e, "gpsimd", s_dve, v)
                e.dma_start(out=out[t * 128:(t + 1) * 128, :], in_=obuf2[bi]).then_inc(s_qst[bi].h, 16)
            emit("gpsimd", f_h)
            ready_done[bi] = (s_qst[bi], s_qst[bi].bump(16))

        A.pop()

        fin_vals = None
        def fin(e):
            for sm, vv in fin_vals:
                e.wait_ge(sm.h, vv)
        emit("gpsimd", fin)

        fin_vals = tuple((sm, sm.n) for sm in dma_sems if sm.n > 0)
        with nc.Block() as block:
            @block.sync
            def _(e):
                for f in prog["sync"]:
                    f(e)

            @block.scalar
            def _(e):
                for f in prog["scalar"]:
                    f(e)
                for f in prog["act"]:
                    f(e)

            @block.tensor
            def _(e):
                for f in prog["tensor"]:
                    f(e)

            @block.vector
            def _(e):
                for f in prog["vector"]:
                    f(e)

            @block.gpsimd
            def _(e):
                for f in prog["gpsimd"]:
                    f(e)

    return nc



def _prep_inputs(L0_attn_hidden_states, L0_attn_scales_q, L0_attn_scales_k,
                 L0_attn_scales_v, L0_attn_scales_o, L0_attn_qw_q, L0_attn_qw_k,
                 L0_attn_qw_v, L0_attn_qw_o, L0_ffn_weight_q3, L0_ffn_weight_norm,
                 L0_ffn_bias):
    x = np.asarray(L0_attn_hidden_states, dtype=np.float32)
    com = {
        "qwq": np.ascontiguousarray(np.asarray(L0_attn_qw_q, dtype=np.int32)),
        "qwk": np.ascontiguousarray(np.asarray(L0_attn_qw_k, dtype=np.int32)),
        "qwv": np.ascontiguousarray(np.asarray(L0_attn_qw_v, dtype=np.int32)),
        "qwo": np.ascontiguousarray(np.asarray(L0_attn_qw_o, dtype=np.int32)),
        "sq": np.repeat(np.asarray(L0_attn_scales_q, dtype=np.float32), GS, axis=0),
        "sk": np.repeat(np.asarray(L0_attn_scales_k, dtype=np.float32), GS, axis=0),
        "sv": np.repeat(np.asarray(L0_attn_scales_v, dtype=np.float32), GS, axis=0),
        "so": np.repeat(np.asarray(L0_attn_scales_o, dtype=np.float32), GS, axis=0),
        "fq3": np.ascontiguousarray(np.asarray(L0_ffn_weight_q3, dtype=np.int8)),
        "fnorm": np.ascontiguousarray(
            np.asarray(L0_ffn_weight_norm, dtype=np.float32).reshape(D, 128)),
        "fbias": np.ascontiguousarray(
            np.broadcast_to(np.asarray(L0_ffn_bias, dtype=np.float32)[None, :], (128, D))),
    }
    per_parity = {}
    for p in (0, 1):
        ck, sk_, cq, sq_, dm, ident = _host_consts(p)
        per_parity[p] = {"cosk": ck, "sink": sk_, "cosq": cq, "sinq": sq_,
                         "dmask": dm, "ident": ident}
    in_maps = []
    for core in range(8):
        b, p = core // 2, core % 2
        qrows = np.concatenate([x[b, (2 * j + p) * 128:(2 * j + p) * 128 + 128, :]
                                for j in range(NSLOT)], axis=0)
        m = {"xq": np.ascontiguousarray(qrows), "xkv": np.ascontiguousarray(x[b])}
        m.update(com)
        m.update(per_parity[p])
        in_maps.append(m)
    return in_maps


def kernel(**inputs):
    in_maps = _prep_inputs(**inputs)
    nc = build_nc(debug=False)
    res = run_bass_kernel_spmd(nc, in_maps, core_ids=list(range(8)))
    outp = np.zeros((B, S, D), dtype=np.float32)
    for core in range(8):
        b, p = core // 2, core % 2
        o = res.results[core]["out"]
        for j in range(NSLOT):
            outp[b, (2 * j + p) * 128:(2 * j + p) * 128 + 128, :] = o[j * 128:(j + 1) * 128, :]
    return outp


# version 06
# speedup vs baseline: 1.6907x; 1.6907x over previous
import numpy as np
import ml_dtypes
import concourse.bass as bass
import concourse.mybir as mybir
from concourse.bass_utils import run_bass_kernel_spmd

F32 = mybir.dt.float32
BF16 = mybir.dt.bfloat16
I32 = mybir.dt.int32
I8 = mybir.dt.int8
AF = mybir.ActivationFunctionType
ALU = mybir.AluOpType
AX = mybir.AxisListType
BF16NP = ml_dtypes.bfloat16

B, S, D = 4, 4096, 4096
NH, NKV, HD = 32, 8, 128
GS = 128
NSLOT, QR = 16, 128
THETA = 10000.0
EPS = 1e-5
ISQ = 1.0 / float(np.sqrt(HD))
NEG = -30000.0

KVT = S // 128
QT_ = 2048 // 128


def _ext(j):
    return 256 * (j + 1)



def _rope_tables():
    inv_freq = 1.0 / THETA ** (np.arange(0, HD, 2, dtype=np.float32) / HD)
    pos = np.arange(S, dtype=np.float32)[:, None]
    freqs = pos * inv_freq[None, :]
    emb = np.concatenate([freqs, freqs], axis=-1)
    return np.cos(emb).astype(np.float32), np.sin(emb).astype(np.float32)


def _host_consts(p):
    cos, sin = _rope_tables()
    sgn = np.where(np.arange(HD) < HD // 2, -1.0, 1.0).astype(np.float32)
    sin_adj = sin * sgn[None, :]
    cosk = np.ascontiguousarray(cos.T)
    sink = np.ascontiguousarray(sin_adj.T)
    qpos = np.concatenate([np.arange((2 * j + p) * 128, (2 * j + p) * 128 + 128)
                           for j in range(NSLOT)])
    cosq = np.ascontiguousarray(cos[qpos].T)
    sinq = np.ascontiguousarray(sin_adj[qpos].T)
    dmask = np.zeros((NSLOT, 128, 512), dtype=np.float32)
    for j in range(NSLOT):
        kcols = np.arange(_ext(j) - 256, _ext(j))
        qrows = (2 * j + p) * 128 + np.arange(128)
        dmask[j, :, 256:] = np.where(kcols[None, :] <= qrows[:, None], 0.0, NEG * ISQ)
    ident = np.eye(128, dtype=np.float32).astype(BF16NP)
    return cosk, sink, cosq, sinq, dmask.astype(BF16NP), ident



class Sem:

    def __init__(self, stack, nc, name):
        self.h = stack.enter_context(nc.semaphore(name))
        self.n = 0

    def bump(self, k=1):
        self.n += k
        return self.n


class Arena:

    def __init__(self, t, nbytes):
        self.t = t
        self.nbytes = nbytes
        self.off = 0
        self.marks = []

    def push(self):
        self.marks.append(self.off)

    def pop(self):
        self.off = self.marks.pop()

    def take(self, shape, dt):
        esz = {F32: 4, BF16: 2, I32: 4, I8: 1}[dt]
        n = int(np.prod(shape[1:])) * esz
        off = (self.off + 511) & ~511
        assert off + n <= self.nbytes, f"SBUF arena overflow: {off + n} > {self.nbytes}"
        self.off = off + n
        v = self.t[:, off:off + n].bitcast(dt)
        if len(shape) > 2:
            pat = " ".join(f"d{i}" for i in range(1, len(shape)))
            kw = {f"d{i}": shape[i] for i in range(1, len(shape))}
            v = v.rearrange(f"p ({pat}) -> p {pat}", **kw)
        if shape[0] != 128:
            v = v[0:shape[0]]
        return v


class Waits:

    def __init__(self):
        self.seen = {}

    def wait(self, eng, key, sem, val):
        if val <= 0:
            return
        if self.seen.get((key, id(sem)), -1) >= val:
            return
        self.seen[(key, id(sem))] = val
        eng.wait_ge(sem.h, val)



def build_nc(debug=False, serialize=False, ser_phases=()):
    import contextlib
    nc = bass.Bass("TRN2", target_bir_lowering=False)

    def dram(name, shape, dt, kind):
        return nc.dram_tensor(name, list(shape), dt, kind=kind).ap()

    scratch_kind = "ExternalOutput" if debug else "Internal"

    xq = dram("xq", (2048, D), F32, "ExternalInput")
    xkv = dram("xkv", (S, D), F32, "ExternalInput")
    qwq = dram("qwq", (D, D), I32, "ExternalInput")
    qwk = dram("qwk", (D, 1024), I32, "ExternalInput")
    qwv = dram("qwv", (D, 1024), I32, "ExternalInput")
    qwo = dram("qwo", (D, D), I32, "ExternalInput")
    sq = dram("sq", (D, D), F32, "ExternalInput")
    sk = dram("sk", (D, 1024), F32, "ExternalInput")
    sv = dram("sv", (D, 1024), F32, "ExternalInput")
    so = dram("so", (D, D), F32, "ExternalInput")
    fq3 = dram("fq3", (524288, 12), I8, "ExternalInput")
    fnorm = dram("fnorm", (D, 128), F32, "ExternalInput")
    fbias = dram("fbias", (128, D), F32, "ExternalInput")
    cosk = dram("cosk", (HD, S), F32, "ExternalInput")
    sink = dram("sink", (HD, S), F32, "ExternalInput")
    cosq = dram("cosq", (HD, 2048), F32, "ExternalInput")
    sinq = dram("sinq", (HD, 2048), F32, "ExternalInput")
    dmask = dram("dmask", (NSLOT, 128, 512), BF16, "ExternalInput")
    ident_in = dram("ident", (128, 128), BF16, "ExternalInput")

    out = dram("out", (2048, D), F32, "ExternalOutput")

    xkvT = dram("xkvT", (D, S), BF16, scratch_kind)
    xqT = dram("xqT", (D, 2048), BF16, scratch_kind)
    kTs = dram("kTs", (NKV, HD, S), BF16, scratch_kind)
    vSs = dram("vSs", (NKV, KVT, 128, HD), BF16, scratch_kind)
    qTs = dram("qTs", (NSLOT, NH, HD, 128), BF16, scratch_kind)
    ctxTs = dram("ctxTs", (NSLOT, NH, HD, 128), BF16, scratch_kind)
    res2s = dram("res2s", (2048, D), F32, scratch_kind)
    out1s = dram("out1s", (2048, D), F32, scratch_kind)
    xn2Ts = dram("xn2T", (D, 2048), BF16, scratch_kind)
    ffns = dram("ffns", (2048, D), F32, scratch_kind)
    if debug:
        xnD = dram("xnD", (48, 128, D), BF16, "ExternalOutput")
        mvD = dram("mvD", (48, 128, 4), F32, "ExternalOutput")

    with contextlib.ExitStack() as stack:
        sb = stack.enter_context(nc.sbuf_tensor("arena", [128, 206 * 1024], I8))
        ps_f = stack.enter_context(nc.psum_tensor("ps", [128, 4096], F32))
        ps_b = ps_f[:].bitcast(BF16)
        A = Arena(sb[:], 206 * 1024)

        def bankf(i, w=512):
            return ps_f[:, i * 512:i * 512 + w]

        def bankb(i, w=1024):
            return ps_b[:, i * 1024:i * 1024 + w]

        s_ld = Sem(stack, nc, "s_ld")
        s_st = Sem(stack, nc, "s_st")
        s_pe = Sem(stack, nc, "s_pe")
        s_dve = Sem(stack, nc, "s_dve")
        s_act = Sem(stack, nc, "s_act")
        s_dve2 = Sem(stack, nc, "s_dve2")
        s_act2 = Sem(stack, nc, "s_act2")
        s_gp = Sem(stack, nc, "s_gp")
        s_misc = Sem(stack, nc, "s_misc")
        s_x = [Sem(stack, nc, f"s_x{i}") for i in range(2)]
        s_stg = [Sem(stack, nc, f"s_stg{i}") for i in range(2)]
        s_dbg = Sem(stack, nc, "s_dbg")
        s_xn = Sem(stack, nc, "s_xn")
        s_kst = Sem(stack, nc, "s_kst")
        s_vst = Sem(stack, nc, "s_vst")
        s_qst = [Sem(stack, nc, f"s_qst{i}") for i in range(2)]
        s_q2 = [Sem(stack, nc, f"s_q2{i}") for i in range(2)]
        s_kvl = [Sem(stack, nc, f"s_kvl{i}") for i in range(2)]
        s_cst = [Sem(stack, nc, f"s_cst{i}") for i in range(2)]
        dma_sems = [s_ld, s_st, s_misc, s_dbg, s_xn, s_kst, s_vst] + s_x + s_stg + s_qst + s_q2 + s_kvl + s_cst

        W = Waits()

        prog = {e: [] for e in ("sync", "scalar", "tensor", "vector", "act", "gpsimd")}

        all_sems = dma_sems + [s_pe, s_dve, s_act, s_dve2]
        cur_phase = ["P"]

        def emit(en, fn):
            if serialize or (cur_phase[0] in ser_phases):
                vals = tuple((sm, sm.n) for sm in all_sems if sm.n > 0)
                def wrapped(e, vals=vals, fn=fn, key=en):
                    for sm, vv in vals:
                        W.wait(e, key, sm, vv)
                    fn(e)
                prog[en].append(wrapped)
            else:
                prog[en].append(fn)

        def phase_barrier(v=None):
            vals = [(sm, sm.n) for sm in dma_sems if sm.n > 0]
            for en in ("sync", "tensor", "vector", "act", "gpsimd"):
                def w(e, vals=tuple(vals), key=en):
                    for sm, vv in vals:
                        W.wait(e, key, sm, vv)
                emit(en, w)

        ident = A.take((128, 128), BF16)
        emit("sync", lambda e: e.dma_start(out=ident, in_=ident_in).then_inc(s_misc.h, 16))
        v_ident = s_misc.bump(16)

        epsc = A.take((128, 1), F32)
        emit("vector", lambda e: e.memset(epsc, EPS).then_inc(s_dve.h, 1))
        v_eps = s_dve.bump()

        cur_phase[0] = 'A'
        A.push()
        xbuf = [A.take((128, D), F32) for _ in range(2)]
        stg = [A.take((128, 32, 128), BF16) for _ in range(2)]
        bst = A.take((128, 48), F32)
        mv = A.take((128, 2), F32)
        rstd = A.take((128, 1), F32)
        nmr = A.take((128, 1), F32)

        ready_x = [None, None]
        ready_norm = [None, None]
        xn = [A.take((128, D), BF16) for _ in range(2)]
        ready_tr = [None, None]
        ready_ev = [None, None]
        ready_out = [None, None]

        ntiles = KVT + QT_
        for t0 in range(ntiles):
            t = (t0 + QT_) % ntiles
            bi = t0 % 2
            src = xkv if t < KVT else xq
            row0 = (t if t < KVT else t - KVT) * 128
            dstT = xkvT if t < KVT else xqT
            dcol = row0

            def ld(e, src=src, row0=row0, bi=bi, prev=ready_norm[bi]):
                if prev is not None:
                    W.wait(e, "sync", *prev)
                e.dma_start(out=xbuf[bi], in_=src[row0:row0 + 128, :]).then_inc(s_x[bi].h, 16)
            emit("sync", ld)
            ready_x[bi] = (s_x[bi], s_x[bi].bump(16))

            def dve1(e, bi=bi, rx=ready_x[bi]):
                W.wait(e, "vector", *rx)
                for c in range(7):
                    e.bn_stats(bst[:, 6 * c:6 * c + 6], xbuf[bi][:, 512 * c:512 * (c + 1)])
                e.bn_stats(bst[:, 42:48], xbuf[bi][:, 3584:4096]).then_inc(s_dve.h, 1)
            emit("vector", dve1)
            v_bn = s_dve.bump()

            def dve1b(e, v=v_bn):
                W.wait(e, "vector", s_dve, v)
                e.bn_aggr(mv[:], bst[:]).then_inc(s_dve.h, 1)
            emit("vector", dve1b)
            v_stats = s_dve.bump()

            def act1(e, v=v_stats):
                W.wait(e, "act", s_dve, v_eps)
                W.wait(e, "act", s_dve, v)
                e.activation(rstd[:], mv[:, 1:2], AF.Sqrt, bias=epsc).then_inc(s_act.h, 1)
            emit("act", act1)
            v_sqrt = s_act.bump()

            def dve2(e, v=v_sqrt):
                W.wait(e, "vector", s_act, v)
                e.reciprocal(rstd[:], rstd[:]).then_inc(s_dve.h, 1)
            emit("vector", dve2)
            v_rcp = s_dve.bump()

            def dve2b(e, v=v_rcp):
                W.wait(e, "vector", s_dve, v)
                e.scalar_tensor_tensor(nmr[:], mv[:, 0:1], -1.0, rstd[:], ALU.mult, ALU.mult).then_inc(s_dve.h, 1)
            emit("vector", dve2b)
            v_rstd = s_dve.bump()

            def act2(e, bi=bi, v=v_rstd, prev_tr=ready_tr[bi]):
                W.wait(e, "act", s_dve, v)
                if prev_tr is not None:
                    W.wait(e, "act", *prev_tr)
                e.activation(xn[bi][:], xbuf[bi], AF.Identity, bias=nmr[:], scale=rstd[:]).then_inc(s_act.h, 1)
            emit("act", act2)
            ready_norm[bi] = (s_act, s_act.bump())

            def pe1(e, bi=bi, vn=ready_norm[bi], prev_ev=ready_ev[bi]):
                W.wait(e, "tensor", s_misc, v_ident)
                W.wait(e, "tensor", *vn)
                if prev_ev is not None:
                    W.wait(e, "tensor", *prev_ev)
                ins = None
                for c in range(32):
                    ins = e.transpose(bankb(4 * bi)[:, 0:0] if False else
                                      ps_b[:, 4 * bi * 1024 + c * 128: 4 * bi * 1024 + c * 128 + 128],
                                      xn[bi][:, c * 128:(c + 1) * 128], ident)
                ins.then_inc(s_pe.h, 1)
            emit("tensor", pe1)
            ready_tr[bi] = (s_pe, s_pe.bump())

            def dve3(e, bi=bi, vt=ready_tr[bi], prev_out=ready_out[bi]):
                W.wait(e, "vector", *vt)
                if prev_out is not None:
                    W.wait(e, "vector", *prev_out)
                e.tensor_copy(stg[bi].rearrange("p a b -> p (a b)"),
                              ps_b[:, 4 * bi * 1024: 4 * bi * 1024 + 4096]).then_inc(s_dve.h, 1)
            emit("vector", dve3)
            ready_ev[bi] = (s_dve, s_dve.bump())

            def st1(e, bi=bi, dstT=dstT, dcol=dcol, ve=ready_ev[bi]):
                W.wait(e, "gpsimd", *ve)
                e.dma_start(out=dstT.rearrange("(c p) r -> p c r", p=128)[:, :, dcol:dcol + 128],
                            in_=stg[bi]).then_inc(s_stg[bi].h, 16)
            emit("gpsimd", st1)
            ready_out[bi] = (s_stg[bi], s_stg[bi].bump(16))

            if debug:
                def dbg(e, t=t, bi=bi, vn=ready_norm[bi], vt=ready_tr[bi]):
                    W.wait(e, "gpsimd", *vt)
                    e.dma_start(out=xnD[t], in_=xn[bi][:]).then_inc(s_dbg.h, 16)
                    with nc.allow_non_contiguous_dma(reason="debug"):
                        e.dma_start(out=mvD[t, :, 0:2], in_=mv[:]).then_inc(s_dbg.h, 16)
                        e.dma_start(out=mvD[t, :, 2:3], in_=rstd[:]).then_inc(s_dbg.h, 16)
                        e.dma_start(out=mvD[t, :, 3:4], in_=nmr[:]).then_inc(s_dbg.h, 16)
                emit("gpsimd", dbg)
                s_dbg.bump(64)

        v_phaseA_store = None
        A.pop()
        phase_barrier()

        cur_phase[0] = 'B'
        A.push()
        wk_sb = A.take((128, 32, 1024), BF16)
        wv_sb = A.take((128, 32, 1024), BF16)
        cktab = A.take((128, S), BF16)
        sktab = A.take((128, S), BF16)
        qbuf = A.take((128, 1024), I32)
        sbuf_s = A.take((128, 1024), F32)
        xnTb = A.take((128, 32, 256), BF16)
        kstg = A.take((128, 8, 256), BF16)
        vstg = A.take((128, 2, 1024), BF16)
        rswp = A.take((128, 512), F32)
        rt1 = A.take((128, 512), F32)
        rt2 = A.take((128, 512), F32)

        for tin, tout in ((cosk, cktab), (sink, sktab)):
            def ldt(e, tin=tin, tout=tout):
                e.dma_start(out=tout, in_=tin).then_inc(s_misc.h, 16)
            emit("gpsimd", ldt)
            s_misc.bump(16)
        v_tabs = s_misc.n

        for qw_in, sc_in, wsb in ((qwk, sk, wk_sb), (qwv, sv, wv_sb)):
            for dmc in range(32):
                def ldw(e, qw_in=qw_in, sc_in=sc_in, dmc=dmc, vfree=s_dve.n):
                    W.wait(e, "sync", s_dve, vfree)
                    e.dma_start(out=qbuf, in_=qw_in[dmc * 128:(dmc + 1) * 128, :]).then_inc(s_ld.h, 16)
                    e.dma_start(out=sbuf_s, in_=sc_in[dmc * 128:(dmc + 1) * 128, :]).then_inc(s_ld.h, 16)
                emit("sync", ldw)
                vl = s_ld.bump(32)

                def dq(e, wsb=wsb, dmc=dmc, vl=vl):
                    W.wait(e, "vector", s_ld, vl)
                    e.scalar_tensor_tensor(wsb[:, dmc, :], qbuf, -8.0, sbuf_s,
                                           ALU.add, ALU.mult).then_inc(s_dve.h, 1)
                emit("vector", dq)
                s_dve.bump()
        v_wkv = s_dve.n

        NRC = 16
        v_xn_used = 0
        kv_store_v = 0
        for rc in range(16):
            def ldx(e, rc=rc, vp=v_xn_used):
                if vp:
                    W.wait(e, "sync", s_pe, vp)
                e.dma_start(out=xnTb,
                            in_=xkvT.rearrange("(c p) r -> p c r", p=128)[:, :, rc * 256:(rc + 1) * 256]
                            ).then_inc(s_xn.h, 16)
            emit("sync", ldx)
            v_xl = s_xn.bump(16)

            for h8 in range(8):
                bk = h8 % 2
                def pek(e, h8=h8, bk=bk, v_xl=v_xl, vw=v_wkv,
                        v_ev=W.seen.get(("pekfree", bk), 0)):
                    W.wait(e, "tensor", s_xn, v_xl)
                    W.wait(e, "tensor", s_dve, vw)
                    if v_ev:
                        W.wait(e, "tensor", s_dve2, v_ev)
                    for dmc in range(32):
                        ins = e.matmul(bankf(bk, 256), wk_sb[:, dmc, h8 * 128:(h8 + 1) * 128],
                                       xnTb[:, dmc, :], start=(dmc == 0), stop=(dmc == 31))
                    ins.then_inc(s_pe.h, 1)
                emit("tensor", pek)
                v_pe_k = s_pe.bump()
                v_xn_used = v_pe_k

                def ropek(e, h8=h8, bk=bk, rc=rc, v_pe_k=v_pe_k,
                          vks=W.seen.get(("kstgfree",), 0)):
                    W.wait(e, "vector", s_misc, v_tabs)
                    W.wait(e, "vector", s_pe, v_pe_k)
                    if vks:
                        W.wait(e, "vector", s_kst, vks)
                    pb = bankf(bk, 256)
                    e.tensor_copy(rswp[0:64, 0:256], pb[64:128, :])
                    e.tensor_copy(rswp[64:128, 0:256], pb[0:64, :])
                    e.tensor_tensor(rt1[:, 0:256], pb, cktab[:, rc * 256:(rc + 1) * 256], ALU.mult)
                    e.tensor_tensor(rt2[:, 0:256], rswp[:, 0:256], sktab[:, rc * 256:(rc + 1) * 256], ALU.mult)
                    e.tensor_tensor(kstg[:, h8, :], rt1[:, 0:256], rt2[:, 0:256], ALU.add).then_inc(s_dve2.h, 1)
                emit("vector", ropek)
                W.seen[("pekfree", bk)] = s_dve2.bump()

            v_rope_done = s_dve2.n

            def stk(e, rc=rc, v=v_rope_done):
                W.wait(e, "gpsimd", s_dve2, v)
                e.dma_start(out=kTs.rearrange("h p r -> p h r")[:, :, rc * 256:(rc + 1) * 256],
                            in_=kstg).then_inc(s_kst.h, 16)
            emit("gpsimd", stk)
            W.seen[("kstgfree",)] = s_kst.bump(16)

            for rt in range(2):
                def pev(e, rt=rt, v_xl=v_xl, vw=v_wkv,
                        v_ev=W.seen.get(("pevfree", rt), 0)):
                    W.wait(e, "tensor", s_xn, v_xl)
                    W.wait(e, "tensor", s_dve, vw)
                    if v_ev:
                        W.wait(e, "tensor", s_dve2, v_ev)
                    for half in range(2):
                        for dmc in range(32):
                            ins = e.matmul(bankf(2 + 2 * rt + half),
                                           xnTb[:, dmc, rt * 128:(rt + 1) * 128],
                                           wv_sb[:, dmc, half * 512:(half + 1) * 512],
                                           start=(dmc == 0), stop=(dmc == 31))
                    ins.then_inc(s_pe.h, 1)
                emit("tensor", pev)
                v_pe_v = s_pe.bump()
                v_xn_used = v_pe_v

                def evv(e, rt=rt, v_pe_v=v_pe_v, vst=W.seen.get(("vstgfree",), 0)):
                    W.wait(e, "vector", s_pe, v_pe_v)
                    if vst:
                        W.wait(e, "vector", s_vst, vst)
                    e.tensor_copy(vstg[:, rt, :], ps_f[:, (2 + 2 * rt) * 512:(4 + 2 * rt) * 512]).then_inc(s_dve2.h, 1)
                emit("vector", evv)
                W.seen[("pevfree", rt)] = s_dve2.bump()
            v_vev = s_dve2.n

            def stv(e, rc=rc, v=v_vev):
                W.wait(e, "gpsimd", s_dve2, v)
                for rt in range(2):
                    e.dma_start(out=vSs[:, rc * 2 + rt].rearrange("h p d -> p h d"),
                                in_=vstg[:, rt].rearrange("p (h d) -> p h d", h=8)).then_inc(s_vst.h, 16)
            emit("gpsimd", stv)
            W.seen[("vstgfree",)] = s_vst.bump(32)

        v_phaseB_store = None
        A.pop()
        phase_barrier()

        cur_phase[0] = 'C'
        A.push()
        xqf = A.take((128, 32, 2048), BF16)
        cqtab = A.take((128, 2048), BF16)
        sqtab = A.take((128, 2048), BF16)
        wqf = [A.take((128, 32, 128), BF16) for _ in range(2)]
        qwb = A.take((128, 32, 128), I32)
        sqb = A.take((128, 32, 128), F32)
        qstg = [A.take((128, 16, 128), BF16) for _ in range(2)]
        cswp = A.take((128, 512), F32)
        ct1 = A.take((128, 512), F32)
        ct2 = A.take((128, 512), F32)

        def ldxq(e):
            e.dma_start(out=xqf, in_=xqT.rearrange("(c p) r -> p c r", p=128)).then_inc(s_xn.h, 16)
        emit("sync", ldxq)
        v_xqf = s_xn.bump(16)

        for tin, tout in ((cosq, cqtab), (sinq, sqtab)):
            def ldt2(e, tin=tin, tout=tout):
                e.dma_start(out=tout, in_=tin).then_inc(s_misc.h, 16)
            emit("gpsimd", ldt2)
            s_misc.bump(16)
        v_qtabs = s_misc.n

        for f in range(NH):
            bi = f % 2
            def ldwq(e, f=f, bi=bi, vfree=W.seen.get(("wqdq",), 0)):
                if vfree:
                    W.wait(e, "sync", s_dve, vfree)
                e.dma_start(out=qwb,
                            in_=qwq.rearrange("(c p) (f j) -> p c f j", p=128, j=128)[:, :, f, :]
                            ).then_inc(s_ld.h, 16)
                e.dma_start(out=sqb,
                            in_=sq.rearrange("(c p) (f j) -> p c f j", p=128, j=128)[:, :, f, :]
                            ).then_inc(s_ld.h, 16)
            emit("sync", ldwq)
            v_lw = s_ld.bump(32)

            def dqq(e, bi=bi, v_lw=v_lw, vused=W.seen.get(("wqused", bi), 0)):
                W.wait(e, "vector", s_ld, v_lw)
                if vused:
                    W.wait(e, "vector", s_pe, vused)
                e.scalar_tensor_tensor(wqf[bi].rearrange("p a b -> p (a b)"),
                                       qwb.rearrange("p a b -> p (a b)"),
                                       -8.0, sqb.rearrange("p a b -> p (a b)"),
                                       ALU.add, ALU.mult).then_inc(s_dve.h, 1)
            emit("vector", dqq)
            v_dq = s_dve.bump()
            W.seen[("wqdq",)] = v_dq

            for qc in range(4):
                bk = qc % 2
                def peq(e, f=f, bi=bi, qc=qc, bk=bk, v_dq=v_dq,
                        vev=W.seen.get(("peqfree", bk), 0)):
                    W.wait(e, "tensor", s_xn, v_xqf)
                    W.wait(e, "tensor", s_dve, v_dq)
                    if vev:
                        W.wait(e, "tensor", s_dve2, vev)
                    for dmc in range(32):
                        ins = e.matmul(bankf(bk), wqf[bi][:, dmc, :],
                                       xqf[:, dmc, qc * 512:(qc + 1) * 512],
                                       start=(dmc == 0), stop=(dmc == 31))
                    ins.then_inc(s_pe.h, 1)
                emit("tensor", peq)
                v_pq = s_pe.bump()
                if qc == 3:
                    W.seen[("wqused", bi)] = v_pq

                def ropeq(e, qc=qc, bk=bk, bi=bi, v_pq=v_pq,
                          vst=W.seen.get(("qstgfree", bi), 0)):
                    W.wait(e, "vector", s_misc, v_qtabs)
                    W.wait(e, "vector", s_pe, v_pq)
                    if vst:
                        W.wait(e, "vector", s_qst[bi], vst)
                    pb = bankf(bk)
                    e.tensor_copy(cswp[0:64, :], pb[64:128, :])
                    e.tensor_copy(cswp[64:128, :], pb[0:64, :])
                    e.tensor_tensor(ct1, pb, cqtab[:, qc * 512:(qc + 1) * 512], ALU.mult)
                    e.tensor_tensor(ct2, cswp, sqtab[:, qc * 512:(qc + 1) * 512], ALU.mult)
                    e.tensor_tensor(qstg[bi].rearrange("p a b -> p (a b)")[:, qc * 512:(qc + 1) * 512],
                                    ct1, ct2, ALU.add).then_inc(s_dve2.h, 1)
                emit("vector", ropeq)
                W.seen[("peqfree", bk)] = s_dve2.bump()
            v_qrope = s_dve2.n

            def stq(e, f=f, bi=bi, v=v_qrope):
                W.wait(e, "gpsimd", s_dve2, v)
                e.dma_start(out=qTs[:, f].rearrange("s p q -> p s q"),
                            in_=qstg[bi]).then_inc(s_qst[bi].h, 16)
            emit("gpsimd", stq)
            W.seen[("qstgfree", bi)] = s_qst[bi].bump(16)

        v_phaseC_store = None
        A.pop()
        phase_barrier()

        cur_phase[0] = 'D'
        A.push()
        qslot = [A.take((128, NH, 128), BF16) for _ in range(2)]
        kbuf = [A.take((128, S), BF16) for _ in range(2)]
        vbuf = [A.take((128, KVT, 128), BF16) for _ in range(2)]
        sP = [A.take((128, S), F32) for _ in range(2)]
        Pex = [A.take((128, S), BF16) for _ in range(2)]
        mxcols = [A.take((128, 8), F32) for _ in range(2)]
        negm = [A.take((128, 1), F32) for _ in range(2)]
        ssum = [A.take((128, 1), F32) for _ in range(2)]
        rsc = [A.take((128, 1), F32) for _ in range(2)]
        PT = A.take((128, KVT, 128), BF16)
        ctxstage = [A.take((128, NH, 128), BF16) for _ in range(2)]
        dm_sb = A.take((128, NSLOT, 512), BF16)
        zer512 = A.take((128, 512), BF16)

        def ldmask(e):
            e.dma_start(out=dm_sb, in_=dmask.rearrange("s p c -> p s c")).then_inc(s_misc.h, 16)
        emit("sync", ldmask)
        v_mask = s_misc.bump(16)
        emit("vector", lambda e: e.memset(zer512, 0.0).then_inc(s_dve.h, 1))
        v_zer = s_dve.bump()

        def stage1(j, h, jp, kp, v_qs, v_kv):
            E = _ext(j)
            nk = (E + 511) // 512
            hp = h % 2
            sb0 = 2 * hp
            for c in range(nk):
                c0 = c * 512
                w = min(512, E - c0)
                bk = sb0 + c % 2
                def sc(e, c0=c0, w=w, bk=bk, v_qs=v_qs, v_kv=v_kv, h=h,
                       jp=jp, kp=kp, vfree=W.seen.get(("scfree", bk), None)):
                    W.wait(e, "tensor", s_q2[jp], v_qs)
                    W.wait(e, "tensor", s_kvl[kp], v_kv)
                    if vfree:
                        W.wait(e, "tensor", vfree[0], vfree[1])
                    e.matmul(bankf(bk, w), qslot[jp][:, h, :], kbuf[kp][:, c0:c0 + w],
                             start=True, stop=True).then_inc(s_pe.h, 1)
                emit("tensor", sc)
                v_sc = s_pe.bump()
                islast = (c == nk - 1)

                if islast or c % 2 == 1:
                    def ttr(e, c=c, c0=c0, w=w, bk=bk, hp=hp, j=j, islast=islast, v_sc=v_sc):
                        W.wait(e, "vector", s_misc, v_mask)
                        W.wait(e, "vector", s_dve, v_zer)
                        W.wait(e, "vector", s_pe, v_sc)
                        m = dm_sb[:, j, 512 - w:512] if islast else zer512[:, 0:w]
                        e.scalar_tensor_tensor(sP[hp][:, c0:c0 + w], bankf(bk, w), ISQ, m,
                                               ALU.mult, ALU.add)
                        e.reduce_max(mxcols[hp][:, c:c + 1], sP[hp][:, c0:c0 + w],
                                     AX.X).then_inc(s_dve.h, 1)
                    emit("vector", ttr)
                    va = s_dve.bump()
                    W.seen[("scfree", bk)] = (s_dve, va)
                else:
                    def eva(e, c0=c0, w=w, bk=bk, hp=hp, v_sc=v_sc):
                        W.wait(e, "act", s_pe, v_sc)
                        e.activation(sP[hp][:, c0:c0 + w], bankf(bk, w), AF.Identity,
                                     scale=ISQ).then_inc(s_act.h, 1)
                    emit("act", eva)
                    vae = s_act.bump()
                    W.seen[("scfree", bk)] = (s_act, vae)
                    def red(e, c=c, c0=c0, w=w, hp=hp, vae=vae):
                        W.wait(e, "vector", s_act, vae)
                        e.reduce_max(mxcols[hp][:, c:c + 1], sP[hp][:, c0:c0 + w],
                                     AX.X).then_inc(s_dve.h, 1)
                    emit("vector", red)
                    s_dve.bump()
            v_ttr_last = s_dve.n

            def mxred(e, hp=hp, nk=nk, v=v_ttr_last):
                W.wait(e, "vector", s_dve, v)
                e.reduce_max(negm[hp], mxcols[hp][:, 0:nk], AX.X, negate=True).then_inc(s_dve.h, 1)
            emit("vector", mxred)
            v_max = s_dve.bump()

            def expop(e, E=E, hp=hp, v_max=v_max, vtr=W.seen.get(("trdone", hp), 0)):
                W.wait(e, "act", s_dve, v_max)
                if vtr:
                    W.wait(e, "act", s_pe, vtr)
                e.activation(Pex[hp][:, 0:E], sP[hp][:, 0:E], AF.Exp, bias=negm[hp],
                             accum_out=ssum[hp]).then_inc(s_act.h, 1)
            emit("act", expop)
            v_exp = s_act.bump()

            def rcp(e, hp=hp, v=v_exp):
                W.wait(e, "vector", s_act, v)
                e.reciprocal(rsc[hp], ssum[hp]).then_inc(s_dve.h, 1)
            emit("vector", rcp)
            v_rcp2 = s_dve.bump()

            def nrm(e, E=E, hp=hp, v=v_rcp2):
                W.wait(e, "vector", s_dve, v)
                e.tensor_scalar(Pex[hp][:, 0:E], Pex[hp][:, 0:E], rsc[hp], None,
                                ALU.mult).then_inc(s_dve.h, 1)
            emit("vector", nrm)
            return s_dve.bump()

        def stage2(j, h, jp, kp, v_nrm):
            E = _ext(j)
            nt = E // 128
            hp = h % 2
            ngr = (nt + 7) // 8
            v_ctx_prev = W.seen.get(("ctxdone",), 0)
            trev = []
            v_tr = None
            for g in range(ngr):
                t0g, t1g = 8 * g, min(8 * g + 8, nt)
                bk = 4 + g % 2
                def trp(e, t0g=t0g, t1g=t1g, bk=bk, hp=hp, v_nrm=v_nrm,
                        vfree=W.seen.get(("trfree", bk), 0)):
                    W.wait(e, "tensor", s_misc, v_ident)
                    W.wait(e, "tensor", s_dve, v_nrm)
                    if vfree:
                        W.wait(e, "tensor", s_act2, vfree)
                    for kt in range(t0g, t1g):
                        ins = e.transpose(bankb(bk)[:, (kt - t0g) * 128:(kt - t0g + 1) * 128],
                                          Pex[hp][:, kt * 128:(kt + 1) * 128], ident)
                    ins.then_inc(s_pe.h, 1)
                emit("tensor", trp)
                v_tr = s_pe.bump()

                def trev_f(e, t0g=t0g, t1g=t1g, bk=bk, v_tr=v_tr, vctx=v_ctx_prev):
                    W.wait(e, "act", s_pe, v_tr)
                    if vctx:
                        W.wait(e, "act", s_pe, vctx)
                    e.activation(PT[:, t0g:t1g, :].rearrange("p a b -> p (a b)"),
                                 bankb(bk)[:, 0:(t1g - t0g) * 128],
                                 AF.Identity).then_inc(s_act2.h, 1)
                emit("act", trev_f)
                vv = s_act2.bump()
                trev.append(vv)
                W.seen[("trfree", bk)] = vv
            W.seen[("trdone", hp)] = v_tr

            def ctxop(e, nt=nt, kp=kp, vtr=trev[-1],
                      vcxfree=W.seen.get(("cxfree",), 0)):
                W.wait(e, "tensor", s_act2, vtr)
                if vcxfree:
                    W.wait(e, "tensor", s_act2, vcxfree)
                for kt in range(nt):
                    ins = e.matmul(bankf(6, 128), vbuf[kp][:, kt, :], PT[:, kt, :],
                                   start=(kt == 0), stop=(kt == nt - 1))
                ins.then_inc(s_pe.h, 1)
            emit("tensor", ctxop)
            v_ctx = s_pe.bump()
            W.seen[("ctxdone",)] = v_ctx
            if h % 4 == 3:
                W.seen[("kvfree", kp)] = v_ctx
            if h == NH - 1:
                W.seen[("qslotfree", jp)] = v_ctx

            def cxev(e, h=h, jp=jp, v=v_ctx,
                     vstg=W.seen.get(("cstgfree", jp), 0)):
                W.wait(e, "act", s_pe, v)
                if vstg:
                    W.wait(e, "act", s_cst[jp], vstg)
                e.activation(ctxstage[jp][:, h, :], bankf(6, 128),
                             AF.Identity).then_inc(s_act2.h, 1)
            emit("act", cxev)
            vce = s_act2.bump()
            W.seen[("cxfree",)] = vce
            if h == NH - 1:
                def stctx(e, j=j, jp=jp, v=s_act2.n):
                    W.wait(e, "gpsimd", s_act2, v)
                    e.dma_start(out=ctxTs[j].rearrange("h d q -> d h q"),
                                in_=ctxstage[jp]).then_inc(s_cst[jp].h, 16)
                emit("gpsimd", stctx)
                W.seen[("cstgfree", jp)] = s_cst[jp].bump(16)

        pending = None
        for j in range(NSLOT):
            E = _ext(j)
            nt = E // 128
            jp = j % 2

            def ldq(e, j=j, jp=jp, vfree=W.seen.get(("qslotfree", j % 2), 0)):
                if vfree:
                    W.wait(e, "sync", s_pe, vfree)
                e.dma_start(out=qslot[jp], in_=qTs[j].rearrange("h p q -> p h q")).then_inc(s_q2[jp].h, 16)
            emit("sync", ldq)
            v_qs = s_q2[jp].bump(16)

            for kvh in range(NKV):
                kp = (j * NKV + kvh) % 2
                def ldkv(e, kvh=kvh, kp=kp, E=E, nt=nt,
                         vfree=W.seen.get(("kvfree", kp), 0)):
                    if vfree:
                        W.wait(e, "sync", s_pe, vfree)
                    e.dma_start(out=kbuf[kp][:, 0:E], in_=kTs[kvh][:, 0:E]).then_inc(s_kvl[kp].h, 16)
                    e.dma_start(out=vbuf[kp][:, 0:nt, :],
                                in_=vSs[kvh, 0:nt].rearrange("t p d -> p t d")).then_inc(s_kvl[kp].h, 16)
                emit("sync", ldkv)
                v_kv = s_kvl[kp].bump(32)

                for hh in range(4):
                    h = kvh * 4 + hh
                    v_nrm = stage1(j, h, jp, kp, v_qs, v_kv)
                    if pending is not None:
                        stage2(*pending)
                    pending = (j, h, jp, kp, v_nrm)
        stage2(*pending)
        pending = None

        v_phaseD_store = None
        A.pop()
        phase_barrier()

        cur_phase[0] = 'O1'
        A.push()
        ctxf = A.take((128, NH, 2048), BF16)
        qob = A.take((128, 32, 256), BF16)
        sob = A.take((128, 32, 256), BF16)
        wof = A.take((128, 32, 256), BF16)
        o1stg = [A.take((128, 256), F32) for _ in range(2)]

        for sl in range(NSLOT):
            def ldcx(e, sl=sl):
                e.dma_start(out=ctxf[:, :, sl * 128:(sl + 1) * 128],
                            in_=ctxTs[sl].rearrange("h d q -> d h q")).then_inc(s_xn.h, 16)
            emit("sync", ldcx)
        v_ctxf = s_xn.bump(16 * NSLOT)

        for jc in range(16):
            def ldwo(e, jc=jc, vfree=W.seen.get(("wodq",), 0)):
                if vfree:
                    W.wait(e, "gpsimd", s_dve, vfree)
                e.dma_start(out=qob,
                            in_=qwo.rearrange("(c p) (f j) -> p c f j", p=128, j=256)[:, :, jc, :]
                            ).then_inc(s_ld.h, 16)
                e.dma_start(out=sob,
                            in_=so.rearrange("(c p) (f j) -> p c f j", p=128, j=256)[:, :, jc, :]
                            ).then_inc(s_ld.h, 16)
            emit("gpsimd", ldwo)
            v_lwo = s_ld.bump(32)

            def dqo(e, v_lwo=v_lwo, vused=W.seen.get(("woused",), 0)):
                W.wait(e, "vector", s_ld, v_lwo)
                if vused:
                    W.wait(e, "vector", s_pe, vused)
                e.scalar_tensor_tensor(wof.rearrange("p a b -> p (a b)"),
                                       qob.rearrange("p a b -> p (a b)"), -8.0,
                                       sob.rearrange("p a b -> p (a b)"),
                                       ALU.add, ALU.mult).then_inc(s_dve.h, 1)
            emit("vector", dqo)
            v_dqo = s_dve.bump()
            W.seen[("wodq",)] = v_dqo

            for qt in range(NSLOT):
                bk = qt % 2
                def peo(e, jc=jc, qt=qt, bk=bk, v_dqo=v_dqo,
                        vev=W.seen.get(("peofree", bk), 0)):
                    W.wait(e, "tensor", s_xn, v_ctxf)
                    W.wait(e, "tensor", s_dve, v_dqo)
                    if vev:
                        W.wait(e, "tensor", s_dve2, vev)
                    for dmc in range(32):
                        ins = e.matmul(bankf(bk, 256), ctxf[:, dmc, qt * 128:(qt + 1) * 128],
                                       wof[:, dmc, :], start=(dmc == 0), stop=(dmc == 31))
                    ins.then_inc(s_pe.h, 1)
                emit("tensor", peo)
                v_po = s_pe.bump()
                if qt == NSLOT - 1:
                    W.seen[("woused",)] = v_po

                def evo(e, bk=bk, qt=qt, v_po=v_po,
                        vst=W.seen.get(("o1free", qt % 2), 0)):
                    W.wait(e, "vector", s_pe, v_po)
                    if vst:
                        W.wait(e, "vector", s_stg[qt % 2], vst)
                    e.tensor_copy(o1stg[qt % 2], bankf(bk, 256)).then_inc(s_dve2.h, 1)
                emit("vector", evo)
                vevo = s_dve2.bump()
                W.seen[("peofree", bk)] = vevo

                def sto(e, jc=jc, qt=qt, v=vevo):
                    W.wait(e, "gpsimd", s_dve2, v)
                    e.dma_start(out=out1s[qt * 128:(qt + 1) * 128, jc * 256:(jc + 1) * 256],
                                in_=o1stg[qt % 2]).then_inc(s_stg[qt % 2].h, 16)
                emit("gpsimd", sto)
                W.seen[("o1free", qt % 2)] = s_stg[qt % 2].bump(16)

        A.pop()
        phase_barrier()

        cur_phase[0] = 'O2'
        A.push()
        obuf = [A.take((128, D), F32) for _ in range(2)]
        xqb = [A.take((128, D), F32) for _ in range(2)]
        stg2 = [A.take((128, 32, 128), BF16) for _ in range(2)]
        r2b = [A.take((128, D), F32) for _ in range(2)]
        xn2b = [A.take((128, D), BF16) for _ in range(2)]
        bst2 = A.take((128, 48), F32)
        mv2 = A.take((128, 2), F32)
        rstd2 = A.take((128, 1), F32)
        nmr2 = A.take((128, 1), F32)
        ln1b = A.take((128, D), F32)

        ready_n2 = [None, None]
        ready_tr2 = [None, None]
        ready_ev2 = [None, None]
        ready_out2 = [None, None]
        ready_r2st = [None, None]

        for t in range(QT_):
            bi = t % 2

            def ld2(e, t=t, bi=bi, prev=ready_n2[bi]):
                if prev is not None:
                    W.wait(e, "sync", *prev)
                e.dma_start(out=obuf[bi], in_=out1s[t * 128:(t + 1) * 128, :]).then_inc(s_x[bi].h, 16)
                e.dma_start(out=xqb[bi], in_=xq[t * 128:(t + 1) * 128, :]).then_inc(s_x[bi].h, 16)
            emit("sync", ld2)
            rx = (s_x[bi], s_x[bi].bump(32))

            def st_a(e, bi=bi, rx=rx):
                W.wait(e, "vector", *rx)
                for c in range(7):
                    e.bn_stats(bst2[:, 6 * c:6 * c + 6], obuf[bi][:, 512 * c:512 * (c + 1)])
                e.bn_stats(bst2[:, 42:48], obuf[bi][:, 3584:4096]).then_inc(s_dve.h, 1)
            emit("vector", st_a)
            v_s0 = s_dve.bump()

            def st_a2(e, v=v_s0):
                W.wait(e, "vector", s_dve, v)
                e.bn_aggr(mv2[:], bst2[:]).then_inc(s_dve.h, 1)
            emit("vector", st_a2)
            v_s1 = s_dve.bump()

            def st_b(e, v=v_s1):
                W.wait(e, "act", s_dve, v)
                e.activation(rstd2[:], mv2[:, 1:2], AF.Sqrt, bias=epsc).then_inc(s_act.h, 1)
            emit("act", st_b)
            v_s2 = s_act.bump()

            def st_c(e, v=v_s2):
                W.wait(e, "vector", s_act, v)
                e.reciprocal(rstd2[:], rstd2[:]).then_inc(s_dve.h, 1)
            emit("vector", st_c)
            v_s3 = s_dve.bump()

            def st_d(e, v=v_s3):
                W.wait(e, "vector", s_dve, v)
                e.scalar_tensor_tensor(nmr2[:], mv2[:, 0:1], -1.0, rstd2[:], ALU.mult,
                                       ALU.mult).then_inc(s_dve.h, 1)
            emit("vector", st_d)
            v_s4 = s_dve.bump()

            def st_e(e, bi=bi, v=v_s4, prevadd=W.seen.get(("ln1free",), 0)):
                W.wait(e, "act", s_dve, v)
                if prevadd:
                    W.wait(e, "act", s_dve, prevadd)
                e.activation(ln1b[:], obuf[bi], AF.Identity, bias=nmr2[:],
                             scale=rstd2[:]).then_inc(s_act.h, 1)
            emit("act", st_e)
            v_ln1 = s_act.bump()

            def st_g(e, bi=bi, v=v_ln1, prevst=ready_r2st[bi], prevn2=ready_n2[bi]):
                W.wait(e, "vector", s_act, v)
                if prevst is not None:
                    W.wait(e, "vector", *prevst)
                e.tensor_tensor(r2b[bi], xqb[bi], ln1b[:], ALU.add).then_inc(s_dve.h, 1)
            emit("vector", st_g)
            v_r2 = s_dve.bump()
            W.seen[("ln1free",)] = v_r2

            def st_h(e, bi=bi, t=t, v=v_r2):
                W.wait(e, "gpsimd", s_dve, v)
                e.dma_start(out=res2s[t * 128:(t + 1) * 128, :], in_=r2b[bi]).then_inc(s_qst[bi].h, 16)
            emit("gpsimd", st_h)
            ready_r2st[bi] = (s_qst[bi], s_qst[bi].bump(16))

            def st_i(e, bi=bi, v=v_r2):
                W.wait(e, "vector", s_dve, v)
                for c in range(7):
                    e.bn_stats(bst2[:, 6 * c:6 * c + 6], r2b[bi][:, 512 * c:512 * (c + 1)])
                e.bn_stats(bst2[:, 42:48], r2b[bi][:, 3584:4096]).then_inc(s_dve.h, 1)
            emit("vector", st_i)
            v_t0 = s_dve.bump()

            def st_i2(e, v=v_t0):
                W.wait(e, "vector", s_dve, v)
                e.bn_aggr(mv2[:], bst2[:]).then_inc(s_dve.h, 1)
            emit("vector", st_i2)
            v_t1 = s_dve.bump()

            def st_j(e, v=v_t1):
                W.wait(e, "act", s_dve, v)
                e.activation(rstd2[:], mv2[:, 1:2], AF.Sqrt, bias=epsc).then_inc(s_act.h, 1)
            emit("act", st_j)
            v_t2 = s_act.bump()

            def st_k(e, v=v_t2):
                W.wait(e, "vector", s_act, v)
                e.reciprocal(rstd2[:], rstd2[:]).then_inc(s_dve.h, 1)
            emit("vector", st_k)
            v_t3 = s_dve.bump()

            def st_l(e, v=v_t3):
                W.wait(e, "vector", s_dve, v)
                e.scalar_tensor_tensor(nmr2[:], mv2[:, 0:1], -1.0, rstd2[:], ALU.mult,
                                       ALU.mult).then_inc(s_dve.h, 1)
            emit("vector", st_l)
            v_t4 = s_dve.bump()

            def st_m(e, bi=bi, v=v_t4, prev_tr=ready_tr2[bi]):
                W.wait(e, "act", s_dve, v)
                if prev_tr is not None:
                    W.wait(e, "act", *prev_tr)
                e.activation(xn2b[bi][:], r2b[bi], AF.Identity, bias=nmr2[:],
                             scale=rstd2[:]).then_inc(s_act.h, 1)
            emit("act", st_m)
            ready_n2[bi] = (s_act, s_act.bump())

            def st_n(e, bi=bi, vn=ready_n2[bi], prev_ev=ready_ev2[bi]):
                W.wait(e, "tensor", s_misc, v_ident)
                W.wait(e, "tensor", *vn)
                if prev_ev is not None:
                    W.wait(e, "tensor", *prev_ev)
                for c in range(32):
                    ins = e.transpose(ps_b[:, 4 * bi * 1024 + c * 128: 4 * bi * 1024 + c * 128 + 128],
                                      xn2b[bi][:, c * 128:(c + 1) * 128], ident)
                ins.then_inc(s_pe.h, 1)
            emit("tensor", st_n)
            ready_tr2[bi] = (s_pe, s_pe.bump())

            def st_o(e, bi=bi, vt=ready_tr2[bi], prev_out=ready_out2[bi]):
                W.wait(e, "vector", *vt)
                if prev_out is not None:
                    W.wait(e, "vector", *prev_out)
                e.tensor_copy(stg2[bi].rearrange("p a b -> p (a b)"),
                              ps_b[:, 4 * bi * 1024: 4 * bi * 1024 + 4096]).then_inc(s_dve.h, 1)
            emit("vector", st_o)
            ready_ev2[bi] = (s_dve, s_dve.bump())

            def st_p(e, bi=bi, t=t, ve=ready_ev2[bi]):
                W.wait(e, "gpsimd", *ve)
                e.dma_start(out=xn2Ts.rearrange("(c p) r -> p c r", p=128)[:, :, t * 128:(t + 1) * 128],
                            in_=stg2[bi]).then_inc(s_stg[bi].h, 16)
            emit("gpsimd", st_p)
            ready_out2[bi] = (s_stg[bi], s_stg[bi].bump(16))

        A.pop()
        phase_barrier()

        cur_phase[0] = 'E'
        A.push()
        xn2f = A.take((128, 32, 2048), BF16)
        fb_sb = A.take((128, D), F32)
        wtp = A.take((128, 32, 512), BF16)
        byts = A.take((128, 1536), I8)
        nrm_sb = A.take((128, 128), F32)
        vals = A.take((128, 4096), I8)
        t1b = A.take((128, 4096), BF16)
        tmp8 = A.take((128, 512), I8)
        festg = [A.take((128, 512), F32) for _ in range(2)]

        def ldxn2(e):
            e.dma_start(out=xn2f, in_=xn2Ts.rearrange("(c p) r -> p c r", p=128)).then_inc(s_xn.h, 16)
            e.dma_start(out=fb_sb, in_=fbias).then_inc(s_xn.h, 16)
        emit("sync", ldxn2)
        v_xn2f = s_xn.bump(32)

        fq3r = fq3.rearrange("(o g) b -> o (g b)", g=128)
        for oc in range(8):
            for ot4 in range(4):
                ot = oc * 4 + ot4
                def ldby(e, ot=ot, vfree=W.seen.get(("unpk",), 0)):
                    if vfree:
                        W.wait(e, "sync", s_dve, vfree)
                    e.dma_start(out=byts, in_=fq3r[ot * 128:(ot + 1) * 128, :]).then_inc(s_ld.h, 16)
                    e.dma_start(out=nrm_sb, in_=fnorm[ot * 128:(ot + 1) * 128, :]).then_inc(s_ld.h, 16)
                emit("sync", ldby)
                v_by = s_ld.bump(32)

                def unpk(e, v_by=v_by, vtr=W.seen.get(("wtrdone",), 0)):
                    W.wait(e, "vector", s_ld, v_by)
                    if vtr:
                        W.wait(e, "vector", s_pe, vtr)
                    b3 = byts.rearrange("p (c three) -> p c three", three=3)
                    v8 = vals.rearrange("p (c eight) -> p c eight", eight=8)
                    b0, b1, b2 = b3[:, :, 0], b3[:, :, 1], b3[:, :, 2]
                    e.tensor_scalar(v8[:, :, 0], b0, 7, None, ALU.bitwise_and)
                    e.tensor_scalar(v8[:, :, 1], b0, 3, 7, ALU.logical_shift_right, ALU.bitwise_and)
                    e.tensor_scalar(tmp8, b1, 1, 2, ALU.bitwise_and, ALU.logical_shift_left)
                    e.tensor_scalar(v8[:, :, 2], b0, 6, 3, ALU.logical_shift_right, ALU.bitwise_and)
                    e.tensor_tensor(v8[:, :, 2], v8[:, :, 2], tmp8, ALU.bitwise_or)
                    e.tensor_scalar(v8[:, :, 3], b1, 1, 7, ALU.logical_shift_right, ALU.bitwise_and)
                    e.tensor_scalar(v8[:, :, 4], b1, 4, 7, ALU.logical_shift_right, ALU.bitwise_and)
                    e.tensor_scalar(tmp8, b2, 3, 1, ALU.bitwise_and, ALU.logical_shift_left)
                    e.tensor_scalar(v8[:, :, 5], b1, 7, 1, ALU.logical_shift_right, ALU.bitwise_and)
                    e.tensor_tensor(v8[:, :, 5], v8[:, :, 5], tmp8, ALU.bitwise_or)
                    e.tensor_scalar(v8[:, :, 6], b2, 2, 7, ALU.logical_shift_right, ALU.bitwise_and)
                    e.tensor_scalar(v8[:, :, 7], b2, 5, 7, ALU.logical_shift_right, ALU.bitwise_and)
                    e.tensor_scalar(t1b, vals, 2.0 / 7.0, -1.0, ALU.mult, ALU.add)
                    wv = t1b.rearrange("p (g k) -> p g k", k=32)
                    nb = nrm_sb.unsqueeze(2).broadcast_to((128, 128, 32))
                    e.tensor_tensor(wv, wv, nb, ALU.mult).then_inc(s_dve.h, 1)
                emit("vector", unpk)
                v_un = s_dve.bump()
                W.seen[("unpk",)] = v_un

                def wtr(e, ot4=ot4, v_un=v_un, vev=W.seen.get(("wtevd",), 0),
                        vmm=W.seen.get(("ffnmmdone",), 0)):
                    W.wait(e, "tensor", s_misc, v_ident)
                    W.wait(e, "tensor", s_dve, v_un)
                    if vev:
                        W.wait(e, "tensor", s_dve2, vev)
                    if ot4 == 0 and vmm:
                        W.wait(e, "tensor", s_pe, vmm)
                    for c in range(32):
                        ins = e.transpose(ps_b[:, (c // 8) * 1024 + (c % 8) * 128:
                                               (c // 8) * 1024 + (c % 8) * 128 + 128],
                                          t1b[:, c * 128:(c + 1) * 128], ident)
                    ins.then_inc(s_pe.h, 1)
                emit("tensor", wtr)
                v_wtr = s_pe.bump()
                W.seen[("wtrdone",)] = v_wtr

                def wtev(e, ot4=ot4, v_wtr=v_wtr, vmm=W.seen.get(("ffnmmdone",), 0)):
                    W.wait(e, "vector", s_pe, v_wtr)
                    if vmm:
                        W.wait(e, "vector", s_pe, vmm)
                    ins = None
                    for b in range(4):
                        ins = e.tensor_copy(wtp[:, 8 * b:8 * b + 8, ot4 * 128:(ot4 + 1) * 128],
                                            ps_b[:, b * 1024:(b + 1) * 1024].rearrange(
                                                "p (a q) -> p a q", a=8))
                    ins.then_inc(s_dve2.h, 1)
                emit("vector", wtev)
                s_dve2.bump()
                W.seen[("wtevd",)] = s_dve2.n
            v_panel = s_dve2.n

            for tt in range(QT_):
                bk = 4 + tt % 2
                def pef(e, oc=oc, tt=tt, bk=bk, v_panel=v_panel,
                        vev=W.seen.get(("fefree", bk), 0)):
                    W.wait(e, "tensor", s_xn, v_xn2f)
                    W.wait(e, "tensor", s_dve2, v_panel)
                    if vev:
                        W.wait(e, "tensor", s_dve2, vev)
                    for ic in range(32):
                        ins = e.matmul(bankf(bk), xn2f[:, ic, tt * 128:(tt + 1) * 128],
                                       wtp[:, ic, :], start=(ic == 0), stop=(ic == 31))
                    ins.then_inc(s_pe.h, 1)
                emit("tensor", pef)
                v_pf = s_pe.bump()
                if tt == QT_ - 1:
                    W.seen[("ffnmmdone",)] = v_pf

                def evf(e, oc=oc, tt=tt, bk=bk, v_pf=v_pf,
                        vst=W.seen.get(("festgfree", tt % 2), 0)):
                    W.wait(e, "vector", s_xn, v_xn2f)
                    W.wait(e, "vector", s_pe, v_pf)
                    if vst:
                        W.wait(e, "vector", s_vst if tt % 2 else s_kst, vst)
                    e.tensor_tensor(festg[tt % 2], bankf(bk), fb_sb[:, oc * 512:(oc + 1) * 512],
                                    ALU.add).then_inc(s_dve2.h, 1)
                emit("vector", evf)
                v_ef = s_dve2.bump()
                W.seen[("fefree", bk)] = v_ef

                def stf(e, oc=oc, tt=tt, v=v_ef):
                    W.wait(e, "gpsimd", s_dve2, v)
                    sm = s_vst if tt % 2 else s_kst
                    e.dma_start(out=ffns[tt * 128:(tt + 1) * 128, oc * 512:(oc + 1) * 512],
                                in_=festg[tt % 2]).then_inc(sm.h, 16)
                emit("gpsimd", stf)
                sm = s_vst if tt % 2 else s_kst
                W.seen[("festgfree", tt % 2)] = sm.bump(16)

        A.pop()
        phase_barrier()

        cur_phase[0] = 'F'
        A.push()
        fbuf = [A.take((128, D), F32) for _ in range(2)]
        r2f = [A.take((128, D), F32) for _ in range(2)]
        obuf2 = [A.take((128, D), F32) for _ in range(2)]
        bst3 = A.take((128, 48), F32)
        mv3 = A.take((128, 2), F32)
        rstd3 = A.take((128, 1), F32)
        nmr3 = A.take((128, 1), F32)
        lnf = A.take((128, D), F32)

        ready_done = [None, None]
        for t in range(QT_):
            bi = t % 2

            def fl(e, t=t, bi=bi, prev=ready_done[bi]):
                if prev is not None:
                    W.wait(e, "sync", *prev)
                e.dma_start(out=fbuf[bi], in_=ffns[t * 128:(t + 1) * 128, :]).then_inc(s_x[bi].h, 16)
                e.dma_start(out=r2f[bi], in_=res2s[t * 128:(t + 1) * 128, :]).then_inc(s_x[bi].h, 16)
            emit("sync", fl)
            rx = (s_x[bi], s_x[bi].bump(32))

            def f_a(e, bi=bi, rx=rx):
                W.wait(e, "vector", *rx)
                for c in range(7):
                    e.bn_stats(bst3[:, 6 * c:6 * c + 6], fbuf[bi][:, 512 * c:512 * (c + 1)])
                e.bn_stats(bst3[:, 42:48], fbuf[bi][:, 3584:4096]).then_inc(s_dve.h, 1)
            emit("vector", f_a)
            v_f0 = s_dve.bump()

            def f_a2(e, v=v_f0):
                W.wait(e, "vector", s_dve, v)
                e.bn_aggr(mv3[:], bst3[:]).then_inc(s_dve.h, 1)
            emit("vector", f_a2)
            v_f1 = s_dve.bump()

            def f_b(e, v=v_f1):
                W.wait(e, "act", s_dve, v)
                e.activation(rstd3[:], mv3[:, 1:2], AF.Sqrt, bias=epsc).then_inc(s_act.h, 1)
            emit("act", f_b)
            v_f2 = s_act.bump()

            def f_c(e, v=v_f2):
                W.wait(e, "vector", s_act, v)
                e.reciprocal(rstd3[:], rstd3[:]).then_inc(s_dve.h, 1)
            emit("vector", f_c)
            v_f3 = s_dve.bump()

            def f_d(e, v=v_f3):
                W.wait(e, "vector", s_dve, v)
                e.scalar_tensor_tensor(nmr3[:], mv3[:, 0:1], -1.0, rstd3[:], ALU.mult,
                                       ALU.mult).then_inc(s_dve.h, 1)
            emit("vector", f_d)
            v_f4 = s_dve.bump()

            def f_e(e, bi=bi, v=v_f4, prevadd=W.seen.get(("lnffree",), 0)):
                W.wait(e, "act", s_dve, v)
                if prevadd:
                    W.wait(e, "act", s_dve, prevadd)
                e.activation(lnf[:], fbuf[bi], AF.Identity, bias=nmr3[:],
                             scale=rstd3[:]).then_inc(s_act.h, 1)
            emit("act", f_e)
            v_lnf = s_act.bump()

            def f_g(e, bi=bi, v=v_lnf, prevst=ready_done[bi]):
                W.wait(e, "vector", s_act, v)
                if prevst is not None:
                    W.wait(e, "vector", *prevst)
                e.tensor_tensor(obuf2[bi], r2f[bi], lnf[:], ALU.add).then_inc(s_dve.h, 1)
            emit("vector", f_g)
            v_o = s_dve.bump()
            W.seen[("lnffree",)] = v_o

            def f_h(e, bi=bi, t=t, v=v_o):
                W.wait(e, "gpsimd", s_dve, v)
                e.dma_start(out=out[t * 128:(t + 1) * 128, :], in_=obuf2[bi]).then_inc(s_qst[bi].h, 16)
            emit("gpsimd", f_h)
            ready_done[bi] = (s_qst[bi], s_qst[bi].bump(16))

        A.pop()

        fin_vals = None
        def fin(e):
            for sm, vv in fin_vals:
                e.wait_ge(sm.h, vv)
        emit("gpsimd", fin)

        fin_vals = tuple((sm, sm.n) for sm in dma_sems if sm.n > 0)
        with nc.Block() as block:
            @block.sync
            def _(e):
                for f in prog["sync"]:
                    f(e)

            @block.scalar
            def _(e):
                for f in prog["scalar"]:
                    f(e)
                for f in prog["act"]:
                    f(e)

            @block.tensor
            def _(e):
                for f in prog["tensor"]:
                    f(e)

            @block.vector
            def _(e):
                for f in prog["vector"]:
                    f(e)

            @block.gpsimd
            def _(e):
                for f in prog["gpsimd"]:
                    f(e)

    return nc



def _prep_inputs(L0_attn_hidden_states, L0_attn_scales_q, L0_attn_scales_k,
                 L0_attn_scales_v, L0_attn_scales_o, L0_attn_qw_q, L0_attn_qw_k,
                 L0_attn_qw_v, L0_attn_qw_o, L0_ffn_weight_q3, L0_ffn_weight_norm,
                 L0_ffn_bias):
    x = np.asarray(L0_attn_hidden_states, dtype=np.float32)
    com = {
        "qwq": np.ascontiguousarray(np.asarray(L0_attn_qw_q, dtype=np.int32)),
        "qwk": np.ascontiguousarray(np.asarray(L0_attn_qw_k, dtype=np.int32)),
        "qwv": np.ascontiguousarray(np.asarray(L0_attn_qw_v, dtype=np.int32)),
        "qwo": np.ascontiguousarray(np.asarray(L0_attn_qw_o, dtype=np.int32)),
        "sq": np.repeat(np.asarray(L0_attn_scales_q, dtype=np.float32), GS, axis=0),
        "sk": np.repeat(np.asarray(L0_attn_scales_k, dtype=np.float32), GS, axis=0),
        "sv": np.repeat(np.asarray(L0_attn_scales_v, dtype=np.float32), GS, axis=0),
        "so": np.repeat(np.asarray(L0_attn_scales_o, dtype=np.float32), GS, axis=0),
        "fq3": np.ascontiguousarray(np.asarray(L0_ffn_weight_q3, dtype=np.int8)),
        "fnorm": np.ascontiguousarray(
            np.asarray(L0_ffn_weight_norm, dtype=np.float32).reshape(D, 128)),
        "fbias": np.ascontiguousarray(
            np.broadcast_to(np.asarray(L0_ffn_bias, dtype=np.float32)[None, :], (128, D))),
    }
    per_parity = {}
    for p in (0, 1):
        ck, sk_, cq, sq_, dm, ident = _host_consts(p)
        per_parity[p] = {"cosk": ck, "sink": sk_, "cosq": cq, "sinq": sq_,
                         "dmask": dm, "ident": ident}
    in_maps = []
    for core in range(8):
        b, p = core // 2, core % 2
        qrows = np.concatenate([x[b, (2 * j + p) * 128:(2 * j + p) * 128 + 128, :]
                                for j in range(NSLOT)], axis=0)
        m = {"xq": np.ascontiguousarray(qrows), "xkv": np.ascontiguousarray(x[b])}
        m.update(com)
        m.update(per_parity[p])
        in_maps.append(m)
    return in_maps


def kernel(**inputs):
    in_maps = _prep_inputs(**inputs)
    nc = build_nc(debug=False)
    res = run_bass_kernel_spmd(nc, in_maps, core_ids=list(range(8)))
    outp = np.zeros((B, S, D), dtype=np.float32)
    for core in range(8):
        b, p = core // 2, core % 2
        o = res.results[core]["out"]
        for j in range(NSLOT):
            outp[b, (2 * j + p) * 128:(2 * j + p) * 128 + 128, :] = o[j * 128:(j + 1) * 128, :]
    return outp
